# version 07
# baseline (speedup 1.0000x reference)
import numpy as np
import ml_dtypes
import concourse.bass as bass
import concourse.mybir as mybir
from concourse.bass_utils import run_bass_kernel_spmd

F32 = mybir.dt.float32
BF16 = mybir.dt.bfloat16
ALU = mybir.AluOpType
ACT = mybir.ActivationFunctionType
AX = mybir.AxisListType.X

NCORES = 8
SEQ = 2048
NSEQ = 2
D = 1024
H = 16
NP = 8
NQC = 16
NKT = 4
EW = 2176
HF = 512
NE = 9


def _bf(x):
    return np.ascontiguousarray(np.asarray(x, dtype=ml_dtypes.bfloat16))


def _host_consts():
    c = {}
    c["cident"] = _bf(np.eye(128))
    c["cidentf"] = np.ascontiguousarray(np.eye(128, dtype=np.float32))
    R = np.zeros((128, 128))
    for blk in range(2):
        b = 64 * blk
        for m in range(32):
            R[b + m + 32, b + m] = -1.0
        for m in range(32, 64):
            R[b + m - 32, b + m] = 1.0
    c["crot"] = _bf(R)
    d_idx = np.arange(128) % 64
    inv = 1.0 / (10000.0 ** (np.arange(0, 64, 2, dtype=np.float64) / 64.0))
    fr = inv[d_idx % 32]
    ang = fr[:, None] * np.arange(SEQ, dtype=np.float64)[None, :]
    c["ccos"] = _bf(np.cos(ang))
    c["csin"] = _bf(np.sin(ang))
    p = np.arange(128)[:, None]
    col = np.arange(EW)[None, :]
    c["cminm"] = np.ascontiguousarray(
        np.minimum(col - p - 2048, 0).astype(np.float32))
    return c


DBG = {}
DUMP = [False]
YONLY = [False]
YMODE = [0]
PH = [5]
QCX = [16]


def build_program():
    nc = bass.Bass("TRN2", target_bir_lowering=False, detect_race_conditions=False)

    dti = {}
    def din(name, shape, dtype=F32):
        dti[name] = nc.dram_tensor(name, list(shape), dtype, kind="ExternalInput").ap()
    din("x", [NSEQ * SEQ, D])
    din("qkvw", [3 * D, D]); din("projw", [D, D]); din("projb", [D])
    din("slopes", [H])
    din("rw", [8, D]); din("ebias", [8])
    din("sw1", [D, HF]); din("sb1", [HF]); din("sw2", [HF, D]); din("sb2", [D])
    din("rw1", [8, D, HF]); din("rb1", [8, HF])
    din("rw2", [8, HF, D]); din("rb2", [8, D])
    din("cident", [128, 128], BF16); din("cidentf", [128, 128], F32)
    din("crot", [128, 128], BF16)
    din("ccos", [128, SEQ], BF16); din("csin", [128, SEQ], BF16)
    din("cminm", [128, EW], F32)
    out_d = nc.dram_tensor("out", [NSEQ * SEQ, D], F32, kind="ExternalOutput").ap()
    w1bf = nc.dram_tensor("w1bf", [NE, 128, 4096], BF16, kind="Internal").ap()
    w2bf = nc.dram_tensor("w2bf", [NE, 128, 4096], BF16, kind="Internal").ap()
    cfdram = nc.dram_tensor("cfdram", [8, SEQ], F32, kind="Internal").ap()
    if DUMP[0]:
        d_attn = nc.dram_tensor("d_attn", [128, NP * SEQ], BF16, kind="ExternalOutput").ap()
        d_x2pre = nc.dram_tensor("d_x2pre", [128, NP * SEQ], BF16, kind="ExternalOutput").ap()
        d_x1 = nc.dram_tensor("d_x1", [128, NP * SEQ], BF16, kind="ExternalOutput").ap()
        d_x3pre = nc.dram_tensor("d_x3pre", [128, NP * SEQ], BF16, kind="ExternalOutput").ap()
        d_cf = nc.dram_tensor("d_cf", [8, SEQ], F32, kind="ExternalOutput").ap()
        d_ht = nc.dram_tensor("d_ht", [128, 4096], BF16, kind="ExternalOutput").ap()
        d_w2 = nc.dram_tensor("d_w2", [128, 4096], BF16, kind="ExternalOutput").ap()
        d_y = nc.dram_tensor("d_y", [128, 2048], F32, kind="ExternalOutput").ap()
        d_ht0 = nc.dram_tensor("d_ht0", [128, 4096], BF16, kind="ExternalOutput").ap()
        d_ltmp = nc.dram_tensor("d_ltmp", [128, 32], F32, kind="ExternalOutput").ap()
        d_sf = nc.dram_tensor("d_sf", [128, 192], F32, kind="ExternalOutput").ap()
        d_lg = nc.dram_tensor("d_lg", [8, 2048], F32, kind="ExternalOutput").ap()
        d_ud = nc.dram_tensor("d_ud", [128, 512], BF16, kind="ExternalOutput").ap()
        d_up = nc.dram_tensor("d_up", [128, 8192], BF16, kind="ExternalOutput").ap()
        dump_tok = [None]

    plan = {k: [] for k in "tvsyg"}
    sems = {k: None for k in "tvsyg"}
    class Ctr:
        def __init__(self, name):
            self.sem = nc.alloc_semaphore(name)
            self.n = 0
    for k in "tvsyg":
        sems[k] = Ctr("sem_" + k)

    def _emit(key, fn, waits, inc, inck):
        for (c, v) in waits:
            assert v is not None, "wait on None count (missing inc=True?)"
        w = tuple((c, v) for (c, v) in waits if v and v > 0)
        ctr = sems[key]
        cnt = None
        if inc:
            ctr.n += inck
            cnt = ctr.n
        def run(eng, fn=fn, w=w, cnt=cnt, ctr=ctr, inck=inck):
            for (c, v) in w:
                eng.wait_ge(c.sem, v)
            ins = fn(eng)
            if cnt is not None:
                ins.then_inc(ctr.sem, inck)
        plan[key].append(run)
        return cnt

    def T(fn, waits=(), inc=False):
        return _emit("t", fn, waits, inc, 1)
    def V(fn, waits=(), inc=True):
        return _emit("v", fn, waits, inc, 1)
    def S(fn, waits=(), inc=True):
        return _emit("s", fn, waits, inc, 1)

    NDMA = 8
    dpool = {k: [Ctr(f"dma_{k}{i}") for i in range(NDMA)] for k in "yg"}
    didx = {"y": 0, "g": 0}

    def _dma(key, fn, waits):
        c = dpool[key][didx[key] % NDMA]
        didx[key] += 1
        for (cc, v) in waits:
            assert v is not None, "dma wait on None count (missing inc=True?)"
        w = tuple((cc, v) for (cc, v) in waits if v and v > 0)
        pre = c.n
        c.n += 16
        tok = (c, c.n)
        def run(eng, fn=fn, w=w, c=c, pre=pre):
            if pre > 0:
                eng.wait_ge(c.sem, pre)
            for (cc, v) in w:
                eng.wait_ge(cc.sem, v)
            fn(eng).then_inc(c.sem, 16)
        plan[key].append(run)
        return tok

    def Y(fn, waits=()):
        return _dma("y", fn, waits)
    def G(fn, waits=()):
        return _dma("g", fn, waits)

    def fence():
        snap = {k: sems[k].n for k in "tvsyg"}
        dsnap = [(c, c.n) for k in "yg" for c in dpool[k]]
        for k in "tvsyg":
            def run(eng, k=k, snap=snap, dsnap=dsnap):
                for j in "tvsyg":
                    if j != k and snap[j] > 0:
                        eng.wait_ge(sems[j].sem, snap[j])
                for (c, v) in dsnap:
                    if v > 0:
                        eng.wait_ge(c.sem, v)
            plan[k].append(run)

    al = nc.alloc_sbuf_tensor
    cident = al("sb_cident", [128, 128], BF16).ap()
    cidentf = al("sb_cidentf", [128, 128], F32).ap()
    crot = al("sb_crot", [128, 128], BF16).ap()
    onescol = al("sb_onescol", [128, 128], F32).ap()
    zrow = al("sb_zrow", [1, 512], BF16).ap()
    ones128 = al("sb_ones128", [128, 1], BF16).ap()
    s8col = al("sb_s8col", [128, H], F32).ap()
    projbc = al("sb_projbc", [128, 8], F32).ap()
    rb1c = al("sb_rb1c", [128, 4 * NE], F32).ap()
    ebias_c = al("sb_ebias_c", [128, 1], F32).ap()
    rowsf = al("sb_rowsf", [128, SEQ], F32).ap()
    rowsb = al("sb_rowsb", [128, SEQ], BF16).ap()
    ltmp = al("sb_ltmp", [128, 32], F32).ap()
    smallf = al("sb_smallf", [128, 192], F32).ap()

    A_q = al("sb_A_q", [128, NP * SEQ], BF16).ap()
    A_k = al("sb_A_k", [128, NP * SEQ], BF16).ap()
    A_v = al("sb_A_v", [128, NP * SEQ], BF16).ap()
    A_a = al("sb_A_a", [128, NP * SEQ], BF16).ap()
    U = al("sb_U", [128, 17408], BF16).ap()
    A_tab = al("sb_A_tab", [128, 2 * SEQ], BF16).ap()
    A_w = al("sb_A_w", [128, 2 * D], BF16).ap()
    A_w2 = al("sb_A_w2", [128, 2 * D], BF16).ap()
    A_bc = al("sb_A_bc", [128, 2 * SEQ], BF16).ap()
    A_cb = A_bc[:, 0:2048]
    A_qf = A_q.bitcast(F32)
    A_ot = A_qf[:, 0:2048]
    A_of = A_qf[:, 2048:2560]

    assert nc.sbuf_bytes_remaining >= 0
    DBG.update(ltmp=ltmp, A_q=A_q, A_k=A_k,
               A_v=A_v, A_a=A_a, U=U, rowsf=rowsf, smallf=smallf,
               s8col=s8col)

    ps = nc.alloc_psum_tensor("ps", [128, 4096], F32).ap()
    psb = ps.bitcast(BF16)
    DBG['ps'] = ps

    def dma_y(dst, src, waits=()):
        return Y(lambda e, d=dst, s=src: e.dma_start(d, s), waits=waits)
    def dma_g(dst, src, waits=()):
        return G(lambda e, d=dst, s=src: e.dma_start(d, s), waits=waits)

    dma_y(cident, dti["cident"])
    dma_y(cidentf, dti["cidentf"])
    dma_y(crot, dti["crot"])
    Y(lambda e: e.dma_start(projbc, dti["projb"].rearrange("(m p) -> p m", p=128),
                            allow_slow_non_contiguous=True))
    dma_y(ebias_c[32:40, 0:1], dti["ebias"].rearrange("(e o) -> e o", o=1))
    y_sl = dma_y(smallf[0:1, 176:192], dti["slopes"].rearrange("(o h) -> o h", o=1))
    mm_ones = V(lambda e: e.memset(onescol, 1.0))
    t_sl = T(lambda e: e.matmul(ps[:, 0:16], onescol[0:1, 0:128],
                                smallf[0:1, 176:192], start=True, stop=True),
             waits=[y_sl, (sems["v"], mm_ones)], inc=True)
    V(lambda e: e.tensor_scalar_mul(s8col, ps[:, 0:16], 8.0),
      waits=[(sems["t"], t_sl)])
    for e_i in range(8):
        Y(lambda e, e_i=e_i: e.dma_start(
            rb1c[:, 4 * e_i:4 * e_i + 4],
            dti["rb1"][e_i, :].rearrange("(m p) -> p m", p=128),
            allow_slow_non_contiguous=True))
    Y(lambda e: e.dma_start(rb1c[:, 32:36],
                            dti["sb1"].rearrange("(m p) -> p m", p=128),
                            allow_slow_non_contiguous=True))
    V(lambda e: e.memset(zrow, 0.0))
    V(lambda e: e.memset(ones128, 1.0))
    V(lambda e: e.memset(ltmp[:], 0.0))

    pre_done = []
    for kind in (1, 2):
        for e_i in range(NE):
            sl = e_i % 2
            stg = A_q[:, 8192 * sl:8192 * sl + 4096]
            if kind == 1:
                src = (dti["rw1"][e_i, :, :] if e_i < 8 else dti["sw1"])
                srcv = src.rearrange("(kd p) c -> p kd c", p=128)
                dstd = w1bf[e_i, :, :]
            else:
                src = (dti["rw2"][e_i, :, :] if e_i < 8 else dti["sw2"])
                srcv = src.rearrange("(kh p) c -> p kh c", p=128)
                dstd = w2bf[e_i, :, :]
            w0 = [pre_done[-1]] if len(pre_done) >= 1 else []
            g1 = dma_g(stg.rearrange("p (a c) -> p a c", a=(8 if kind == 1 else 4)),
                       srcv, waits=w0)
            g2 = dma_g(dstd, stg, waits=[g1])
            pre_done.append(g2)
    precast_done = pre_done[-1]
    fence()

    out2 = out_d.rearrange("(s t) d -> s t d", s=NSEQ)
    x2 = dti["x"].rearrange("(s t) d -> s t d", s=NSEQ)

    tab_cos = A_tab[:, 0:SEQ]
    tab_sin = A_tab[:, SEQ:2 * SEQ]

    U_P = [U[:, SEQ * i:SEQ * i + SEQ] for i in range(4)]
    U_PT = [U[:, 8192 + SEQ * i:8192 + SEQ * i + SEQ] for i in range(2)]
    U_D = [U[:, 12288 + 128 * i:12288 + 128 * i + 128] for i in range(4)]
    U_x = U[:, 0:NP * SEQ]
    U_h8 = U[:, 0:4096]
    U_w2db = [U[:, 4096:4096 + 2048], U[:, 6144:6144 + 2048]]

    def htil_view(e_i):
        if e_i < 4:
            return A_q[:, 4096 * e_i:4096 * e_i + 4096]
        if e_i < 8:
            return A_k[:, 4096 * (e_i - 4):4096 * (e_i - 4) + 4096]
        return U_h8

    w1_free = {0: None, 1: None}
    cb_free = {0: None, 1: None}
    w2_free = {0: None, 1: None}

    nseq_run = NSEQ if PH[0] >= 5 else 1
    for s in range(nseq_run):
        w_tab = None
        for tsl, tdram in ((tab_cos, "ccos"), (tab_sin, "csin")):
            w_tab = dma_y(tsl, dti[tdram])

        def load_xfm():
            tload = {}
            vev = {}
            for tc in range(NQC):
                slot = A_w[:, D * (tc % 2):D * (tc % 2) + D]
                w0 = [(sems["t"], tload[tc - 2])] if tc >= 2 else []
                g_ld = dma_g(slot, x2[s, 128 * tc:128 * tc + 128, :], waits=w0)
                last = None
                for kd in range(8):
                    w1_ = []
                    if kd == 0:
                        w1_.append(g_ld)
                        if tc >= 1:
                            w1_.append((sems["v"], vev[tc - 1]))
                    last = T(lambda e, slot=slot, kd=kd: e.transpose(
                        psb[:, 128 * kd:128 * kd + 128],
                        slot[:, 128 * kd:128 * kd + 128], cident),
                        waits=w1_, inc=(kd == 7))
                tload[tc] = last
                dstv = U_x.rearrange("p (kd t) -> p kd t", kd=8)[:, :, 128 * tc:128 * tc + 128]
                srcv = psb[:, 0:1024].rearrange("p (kd t) -> p kd t", kd=8)
                vev[tc] = V(lambda e, o=dstv, i=srcv: e.tensor_copy(o, i),
                            waits=[(sems["t"], last)])
            return vev[NQC - 1]

        x_ready = load_xfm()
        fence()

        wnat = [A_w[:, 0:D], A_w[:, D:2 * D]]
        wT = [A_w2[:, 0:D], A_w2[:, D:2 * D]]
        wT_ev = {}
        wT_last = {}
        mm_last = {}
        bank_ev = {}
        prev_rot = None
        prev_t2 = None
        prev_vev = None
        prev_nm = None
        prev_ntp = None
        prev_vtp = None
        for mo in range(24):
            sl = mo % 2
            w0 = [(sems["t"], wT_last[mo - 2])] if mo >= 2 else []
            g_w = dma_g(wnat[sl], dti["qkvw"][128 * mo:128 * mo + 128, :], waits=w0)
            lw = None
            for kd in range(8):
                w1_ = []
                if kd == 0:
                    w1_.append(g_w)
                    if mo >= 2:
                        w1_.append((sems["v"], wT_ev[mo - 2]))
                lw = T(lambda e, sl=sl, kd=kd: e.transpose(
                    psb[:, 3072 + 128 * kd:3072 + 128 * kd + 128],
                    wnat[sl][:, 128 * kd:128 * kd + 128], cident),
                    waits=w1_, inc=(kd == 7))
            wev = [(sems["t"], lw)]
            if mo - 2 in mm_last:
                wev.append((sems["t"], mm_last[mo - 2]))
            if mo < 8:
                ev1 = V(lambda e, sl=sl: e.tensor_scalar_mul(
                    wT[sl], psb[:, 3072:4096], 0.125), waits=wev)
            else:
                ev1 = V(lambda e, sl=sl: e.tensor_copy(
                    wT[sl], psb[:, 3072:4096]), waits=wev)
            wT_ev[mo] = ev1
            wT_last[mo] = lw

            p_i = mo % 8
            kind = "q" if mo < 8 else ("k" if mo < 16 else "v")
            store = A_q if kind == "q" else (A_k if kind == "k" else None)
            for nt in range(NKT):
                bank = 512 * (nt % 2)
                w0 = [(sems["v"], ev1)]
                if (nt % 2) in bank_ev:
                    w0.append((sems["v"], bank_ev[nt % 2]))
                if mo == 0 and nt == 0:
                    w0.append((sems["v"], x_ready))
                last = None
                for kd in range(8):
                    last = T(lambda e, bank=bank, kd=kd, sl=sl, nt=nt: e.matmul(
                        ps[:, bank:bank + 512],
                        wT[sl][:, 128 * kd:128 * kd + 128],
                        U_x[:, SEQ * kd + 512 * nt:SEQ * kd + 512 * nt + 512],
                        start=(kd == 0), stop=(kd == 7)),
                        waits=(w0 if kd == 0 else []), inc=(kd == 7))

                mm_last[mo] = last
                if kind in ("q", "k"):
                    qraw = rowsb[:, 0:512]
                    t1b = rowsb[:, 512:1024]
                    t2b = rowsb[:, 1024:1536]
                    sqb = rowsb[:, 1536:2048]
                    wv = [(sems["t"], last)]
                    if prev_rot is not None:
                        wv.append((sems["t"], prev_rot))
                    cpy = V(lambda e, bank=bank, q=qraw: e.tensor_copy(
                        q, ps[:, bank:bank + 512]), waits=wv)
                    bank_ev[nt % 2] = cpy
                    wrot = [(sems["v"], cpy)]
                    if prev_t2 is not None:
                        wrot.append((sems["v"], prev_t2))
                    rot = T(lambda e, q=qraw: e.matmul(
                        ps[:, 1024:1536], crot, q, start=True, stop=True),
                        waits=wrot, inc=True)
                    prev_rot = rot
                    ctab = tab_cos
                    stab = tab_sin
                    V(lambda e, q=qraw, ct=ctab, nt=nt, t1=t1b: e.tensor_tensor(
                        t1, q, ct[:, 512 * nt:512 * nt + 512], ALU.mult),
                      waits=[w_tab])
                    prev_t2 = V(lambda e, st=stab, nt=nt, t2=t2b: e.scalar_tensor_tensor(
                        t2, ps[:, 1024:1536], 1.0, st[:, 512 * nt:512 * nt + 512],
                        ALU.mult, ALU.mult),
                      waits=[(sems["t"], rot)])
                    dst = store[:, SEQ * p_i + 512 * nt:SEQ * p_i + 512 * nt + 512]
                    vq_ev_dst = V(lambda e, d=dst, t1=t1b, t2=t2b: e.tensor_tensor(
                        d, t1, t2, ALU.add))
                else:
                    vraw = rowsb[:, 0:512]
                    wv = [(sems["t"], last)]
                    if prev_vtp is not None:
                        wv.append((sems["t"], prev_vtp))
                    cpy = V(lambda e, bank=bank, q=vraw: e.tensor_copy(
                        q, ps[:, bank:bank + 512]), waits=wv)
                    bank_ev[nt % 2] = cpy
                    vtp = None
                    for i4 in range(4):
                        wv2 = []
                        if i4 == 0:
                            wv2.append((sems["v"], cpy))
                            if prev_vev is not None:
                                wv2.append((sems["v"], prev_vev))
                        vtp = T(lambda e, i4=i4: e.transpose(
                            psb[:, 4096 + 128 * i4:4096 + 128 * i4 + 128],
                            vraw[:, 128 * i4:128 * i4 + 128], cident),
                            waits=wv2, inc=(i4 == 3))
                    prev_vtp = vtp
                    dstv = A_v.rearrange("p (tc d) -> p tc d", tc=NQC)[
                        :, 4 * nt:4 * nt + 4, 128 * p_i:128 * p_i + 128]
                    srcv = psb[:, 4096:4608].rearrange("p (i4 d) -> p i4 d", i4=4)
                    prev_vev = V(lambda e, o=dstv, i=srcv: e.tensor_copy(o, i),
                      waits=[(sems["t"], vtp)])
        fence()

        skip_attn = PH[0] < 2
        g_mm1 = dma_y(rowsf, dti["cminm"][:, 0:2048])
        g_mm2 = dma_y(smallf[:, 0:128], dti["cminm"][:, 2048:2176])
        prev_exp2 = None
        prev_bm = [None]
        pt_bank_free = {2560: None, 3584: None}
        last_pve = None
        pv_last = None
        for p_i in range(NP if not skip_attn else 0):
            qk_inc = {}
            exp_inc = {}
            vD_inc = {}
            pt_inc = {}
            ptev_inc = {}
            pv_inc = {}
            pve_inc = {}

            def emit_qk(j, qc):
                base = 64 * j
                stat = A_q[base:base + 64, SEQ * p_i + 128 * qc:SEQ * p_i + 128 * qc + 128]
                w0 = [(sems["v"], prev_bm[0])] if prev_bm[0] else []
                if p_i == 0 and qc == 0 and j == 0:
                    w0.append(g_mm1)
                    w0.append(g_mm2)
                last = None
                for kt in range(NKT):
                    mov = A_k[base:base + 64, SEQ * p_i + 512 * kt:SEQ * p_i + 512 * kt + 512]
                    last = T(lambda e, stat=stat, mov=mov, kt=kt: e.matmul(
                        ps[:, 512 * kt:512 * kt + 512], stat, mov,
                        start=True, stop=True),
                        waits=(w0 if kt == 0 else []), inc=(kt == 3))
                qk_inc[(j, qc)] = last

            def emit_bias_max(j, qc):
                h = 2 * p_i + j
                par = qc % 2
                pbuf = U_P[2 * j + par]
                s8 = s8col[:, h:h + 1]
                w0 = [(sems["t"], qk_inc[(j, qc)])]
                if (j, qc - 2, 3) in pt_inc:
                    w0.append((sems["t"], pt_inc[(j, qc - 2, 3)]))
                if (j, qc - 2) in vD_inc:
                    w0.append((sems["v"], vD_inc[(j, qc - 2)]))
                wp = 128 * qc
                if qc > 0:
                    V(lambda e, pb=pbuf, w=wp, s8=s8: e.scalar_tensor_tensor(
                        pb[:, 0:w], rowsf[:, 2048 - w:2048], s8, ps[:, 0:w],
                        ALU.mult, ALU.add), waits=w0)
                    w0 = []
                V(lambda e, pb=pbuf, w=wp, s8=s8: e.scalar_tensor_tensor(
                    pb[:, w:w + 128], smallf[:, 0:128], s8, ps[:, w:w + 128],
                    ALU.mult, ALU.add), waits=w0)
                if wp + 128 < SEQ:
                    V(lambda e, pb=pbuf, w=wp: e.tensor_copy(
                        pb[:, w + 128:SEQ], ps[:, w + 128:SEQ]))
                bm_inc[(j, qc)] = sems["v"].n
                prev_bm[0] = bm_inc[(j, qc)]

            def emit_exp(j, qc):
                nonlocal prev_exp2
                par = qc % 2
                pbuf = U_P[2 * j + par]
                l0 = 16 * par + 8 * j
                wp = [(sems["v"], bm_inc[(j, qc)])]
                s_l1 = S(lambda e, pb=pbuf, l0=l0: e.activation(
                    rowsb[:, 0:SEQ], pb[:, 0:SEQ], ACT.Exp, scale=0.25,
                    accum_out=ltmp[:, l0:l0 + 1]), waits=wp)
                V(lambda e, l0=l0: e.tensor_tensor(
                    ltmp[:, l0 + 2:l0 + 3], ltmp[:, l0:l0 + 1],
                    ltmp[:, l0 + 1:l0 + 2], ALU.add),
                  waits=[(sems["s"], s_l1)])
                v_l1 = sems["v"].n
                sln = S(lambda e, l0=l0: e.activation(
                    ltmp[:, l0 + 3:l0 + 4], ltmp[:, l0 + 2:l0 + 3], ACT.Ln),
                    waits=[(sems["v"], v_l1)])
                V(lambda e, l0=l0: e.tensor_scalar(
                    ltmp[:, l0 + 3:l0 + 4], ltmp[:, l0 + 3:l0 + 4],
                    -4.0, None, ALU.mult), waits=[(sems["s"], sln)])
                v_negm = sems["v"].n
                negm = ltmp[:, l0 + 3:l0 + 4]
                e2 = S(lambda e, pb=pbuf, b=negm, l0=l0:
                    e.activation(pb[:, 0:SEQ], pb[:, 0:SEQ], ACT.Exp, bias=b,
                                 accum_out=ltmp[:, l0 + 4:l0 + 5]),
                    waits=[(sems["v"], v_negm)])
                exp_inc[(j, qc, 1)] = e2
                exp_inc[(j, qc, 2)] = e2
                prev_exp2 = e2

            def emit_l_D(j, qc):
                par = qc % 2
                l0 = 16 * par + 8 * j
                V(lambda e, l0=l0: e.tensor_tensor(
                    ltmp[:, l0 + 6:l0 + 7], ltmp[:, l0 + 4:l0 + 5],
                    ltmp[:, l0 + 5:l0 + 6], ALU.add),
                  waits=[(sems["s"], exp_inc[(j, qc, 2)])])
                V(lambda e, l0=l0: e.memset(ltmp[:, l0 + 7:l0 + 8], 0.0))
                V(lambda e, l0=l0: e.reciprocal(
                    ltmp[:, l0 + 6:l0 + 7], ltmp[:, l0 + 6:l0 + 7]))
                V(lambda e, l0=l0: e.memset(ltmp[:, l0 + 7:l0 + 8], 0.0))
                dbuf = U_D[2 * j + par]
                wd = []
                if (j, qc - 2, 3) in pt_inc:
                    wd.append((sems["t"], pt_inc[(j, qc - 2, 3)]))
                vD_inc[(j, qc)] = V(lambda e, d=dbuf, l0=l0: e.tensor_scalar(
                    d, cident, ltmp[:, l0 + 6:l0 + 7], None, ALU.mult), waits=wd)

            def emit_pt(j, qc):
                par = qc % 2
                pbuf = U_P[2 * j + par]
                dbuf = U_D[2 * j + par]
                for b in range(4):
                    bank = 2560 if b % 2 == 0 else 3584
                    w0 = []
                    if b == 0:
                        w0.append((sems["v"], vD_inc[(j, qc)]))
                        if qc >= 1 and (qc - 1) in pv_inc:
                            w0.append((sems["t"], pv_inc[qc - 1]))
                    if b >= 2:
                        w0.append((sems["v"], ptev_inc[(j, qc, b - 2)]))
                    elif pt_bank_free[bank] is not None:
                        w0.append((sems["v"], pt_bank_free[bank]))
                    last = None
                    for i4 in range(4):
                        kc = 4 * b + i4
                        last = T(lambda e, pb=pbuf, d=dbuf, kc=kc, bank=bank, i4=i4:
                                 e.matmul(ps[:, bank + 128 * i4:bank + 128 * i4 + 128],
                                          pb[:, 128 * kc:128 * kc + 128], d,
                                          start=True, stop=True),
                                 waits=(w0 if i4 == 0 else []), inc=(i4 == 3))
                    pt_inc[(j, qc, b)] = last
                    dst = U_PT[j][:, 512 * b:512 * b + 512]
                    ev = V(lambda e, o=dst, bank=bank: e.tensor_copy(
                        o, ps[:, bank:bank + 512]),
                        waits=[(sems["t"], last)])
                    ptev_inc[(j, qc, b)] = ev
                    pt_bank_free[bank] = ev

            def emit_pv(qc):
                slot = 128 * (qc % 4)
                w0 = [(sems["v"], ptev_inc[(0, qc, 3)]),
                      (sems["v"], ptev_inc[(1, qc, 3)])]
                if last_pve_l[0] is not None:
                    w0.append((sems["v"], last_pve_l[0]))
                last = None
                for kc in range(NQC):
                    vA = A_v[:, 1024 * kc + 64 * (2 * p_i):1024 * kc + 64 * (2 * p_i) + 64]
                    vB = A_v[:, 1024 * kc + 64 * (2 * p_i) + 64:1024 * kc + 64 * (2 * p_i) + 128]
                    T(lambda e, vA=vA, kc=kc, slot=slot: e.matmul(
                        ps[0:64, 2048 + slot:2048 + slot + 128], vA,
                        U_PT[0][:, 128 * kc:128 * kc + 128],
                        start=(kc == 0), stop=(kc == NQC - 1), tile_position=(0, 0)),
                      waits=(w0 if kc == 0 else []))
                    last = T(lambda e, vB=vB, kc=kc, slot=slot: e.matmul(
                        ps[64:128, 3072 + slot:3072 + slot + 128], vB,
                        U_PT[1][:, 128 * kc:128 * kc + 128],
                        start=(kc == 0), stop=(kc == NQC - 1), tile_position=(0, 64)),
                        inc=(kc == NQC - 1))
                pv_inc[qc] = last

            def emit_pvev(qc):
                slot = 128 * (qc % 4)
                col = SEQ * p_i + 128 * qc
                V(lambda e, slot=slot, col=col: e.tensor_copy(
                    A_a[0:64, col:col + 128],
                    ps[0:64, 2048 + slot:2048 + slot + 128]),
                  waits=[(sems["t"], pv_inc[qc])])
                pve_inc[qc] = V(lambda e, slot=slot, col=col: e.tensor_copy(
                    A_a[64:128, col:col + 128],
                    ps[64:128, 3072 + slot:3072 + slot + 128]))
                last_pve_l[0] = pve_inc[qc]

            last_pve_l = [last_pve]
            bm_inc = {}
            NQ_ = QCX[0]
            for qc in range(NQ_ + 1):
                for j in range(2):
                    if qc < NQ_:
                        emit_qk(j, qc)
                        emit_bias_max(j, qc)
                        emit_exp(j, qc)
                        emit_l_D(j, qc)
                    if qc >= 1:
                        emit_pt(j, qc - 1)
                if qc >= 1:
                    emit_pv(qc - 1)
                    emit_pvev(qc - 1)
            pv_last = pv_inc[NQ_ - 1]
            last_pve = last_pve_l[0]
        fence()

        if DUMP[0] and s == 0:
            dma_y(d_attn, A_a)
            dma_y(d_ltmp, ltmp)
            dma_y(d_ud, U[:, 12288:12800])
            dma_y(d_up, U[:, 0:8192])
        if PH[0] < 3:
            dma_g(out2[s, 0:128, :], A_a[:, 0:1024])
            plan["y"].append(lambda e: None)
            break
        x_ready = load_xfm()
        fence()
        x2pre = A_v
        wT_ev = {}
        wT_last = {}
        mm_last = {}
        bank_ev = {}
        last_proj_ev = None
        for mo in range(8):
            sl = mo % 2
            w0 = [(sems["t"], wT_last[mo - 2])] if mo >= 2 else []
            g_w = dma_g(wnat[sl], dti["projw"][128 * mo:128 * mo + 128, :], waits=w0)
            lw = None
            for kd in range(8):
                w1_ = []
                if kd == 0:
                    w1_.append(g_w)
                    if mo >= 2:
                        w1_.append((sems["v"], wT_ev[mo - 2]))
                    if mo == 0:
                        w1_.append((sems["v"], x_ready))
                lw = T(lambda e, sl=sl, kd=kd: e.transpose(
                    psb[:, 3072 + 128 * kd:3072 + 128 * kd + 128],
                    wnat[sl][:, 128 * kd:128 * kd + 128], cident),
                    waits=w1_, inc=(kd == 7))
            wev = [(sems["t"], lw)]
            if mo - 2 in mm_last:
                wev.append((sems["t"], mm_last[mo - 2]))
            ev1 = V(lambda e, sl=sl: e.tensor_copy(
                wT[sl], psb[:, 3072:4096]), waits=wev)
            wT_ev[mo] = ev1
            wT_last[mo] = lw
            for nt in range(NKT):
                bank = 512 * (nt % 2)
                w0 = [(sems["v"], ev1)]
                if (nt % 2) in bank_ev:
                    w0.append((sems["v"], bank_ev[nt % 2]))
                last = None
                for kd in range(8):
                    last = T(lambda e, bank=bank, kd=kd, sl=sl, nt=nt: e.matmul(
                        ps[:, bank:bank + 512],
                        wT[sl][:, 128 * kd:128 * kd + 128],
                        A_a[:, SEQ * kd + 512 * nt:SEQ * kd + 512 * nt + 512],
                        start=(kd == 0), stop=(kd == 7)),
                        waits=(w0 if kd == 0 else []), inc=(kd == 7))
                mm_last[mo] = last
                dst = x2pre[:, SEQ * mo + 512 * nt:SEQ * mo + 512 * nt + 512]
                ev = V(lambda e, d=dst, bank=bank, mo=mo, nt=nt: e.scalar_tensor_tensor(
                    d, ps[:, bank:bank + 512], projbc[:, mo:mo + 1],
                    U_x[:, SEQ * mo + 512 * nt:SEQ * mo + 512 * nt + 512],
                    ALU.add, ALU.add),
                    waits=[(sems["t"], last)])
                bank_ev[nt % 2] = ev
                last_proj_ev = ev
        fence()

        if DUMP[0] and s == 0:
            dma_y(d_x2pre, A_v)
        def emit_ln(src_, dst, to_out=False):
            bank_free = {}
            sq_slot_mm = {}
            bca = {}
            for nt in range(NKT):
                lastx = None
                for kd in range(8):
                    w0 = []
                    if kd == 0 and nt >= 1:
                        w0.append((sems["v"], bca[(1, nt - 1)]))
                    lastx = T(lambda e, nt=nt, kd=kd, src=src_: e.matmul(
                        ps[32:33, 0:512], ones128,
                        src_[:, SEQ * kd + 512 * nt:SEQ * kd + 512 * nt + 512],
                        start=(kd == 0), stop=(kd == 7), tile_position=(0, 32)),
                        waits=w0, inc=(kd == 7))
                lasts = None
                for kd in range(8):
                    sslot = rowsb[:, 512 * (kd % 2):512 * (kd % 2) + 512]
                    wv = []
                    if (kd % 2) in sq_slot_mm:
                        wv.append((sems["t"], sq_slot_mm[kd % 2]))
                    sqe = V(lambda e, kd=kd, nt=nt, src=src_, sl_=sslot: e.tensor_tensor(
                        sl_, src_[:, SEQ * kd + 512 * nt:SEQ * kd + 512 * nt + 512],
                        src_[:, SEQ * kd + 512 * nt:SEQ * kd + 512 * nt + 512],
                        ALU.mult), waits=wv)
                    lasts = T(lambda e, kd=kd, sl_=sslot: e.matmul(
                        ps[32:33, 512:1024], ones128, sl_,
                        start=(kd == 0), stop=(kd == 7), tile_position=(0, 32)),
                        waits=[(sems["v"], sqe)], inc=True)
                    sq_slot_mm[kd % 2] = lasts
                evx = V(lambda e: e.tensor_copy(rowsf[32:33, 0:512], ps[32:33, 0:512]),
                        waits=[(sems["t"], lastx)])
                evs = V(lambda e: e.tensor_copy(rowsf[32:33, 512:1024],
                                                ps[32:33, 512:1024]),
                        waits=[(sems["t"], lasts)])
                V(lambda e: e.tensor_scalar(rowsf[32:33, 0:512], rowsf[32:33, 0:512],
                                            1.0 / D, None, ALU.mult))
                V(lambda e: e.tensor_tensor(rowsf[32:33, 1024:1536],
                                            rowsf[32:33, 0:512],
                                            rowsf[32:33, 0:512], ALU.mult))
                V(lambda e: e.scalar_tensor_tensor(
                    rowsf[32:33, 512:1024], rowsf[32:33, 512:1024], 1.0 / D,
                    rowsf[32:33, 1024:1536], ALU.mult, ALU.subtract))
                V(lambda e: e.tensor_scalar(rowsf[32:33, 512:1024],
                                            rowsf[32:33, 512:1024],
                                            1e-5, None, ALU.add))
                rc = V(lambda e: e.reciprocal(rowsf[32:33, 512:1024],
                                              rowsf[32:33, 512:1024]))
                st = S(lambda e: e.activation(rowsf[32:33, 512:1024],
                                              rowsf[32:33, 512:1024], ACT.Sqrt),
                       waits=[(sems["v"], rc)])
                for ridx in (0, 1):
                    w0 = [(sems["v"], rc)] if ridx == 0 else [(sems["s"], st)]
                    if (ridx, nt - 1) in bca:
                        w0.append((sems["v"], bca[(ridx, nt - 1)]))
                    lastb = T(lambda e, nt=nt, ridx=ridx: e.matmul(
                        ps[:, 1024 + 512 * ridx:1024 + 512 * ridx + 512],
                        onescol[32:33, :], rowsf[32:33, 512 * ridx:512 * ridx + 512],
                        start=True, stop=True), waits=w0, inc=True)
                    bca[(ridx, nt)] = V(lambda e, ridx=ridx, nt=nt: e.tensor_copy(
                        A_bc[:, SEQ * ridx + 512 * nt:SEQ * ridx + 512 * nt + 512],
                        ps[:, 1024 + 512 * ridx:1024 + 512 * ridx + 512]),
                        waits=[(sems["t"], lastb)])
            meanb = A_bc[:, 0:SEQ]
            rstdb = A_bc[:, SEQ:2 * SEQ]
            wlast = (sems["v"], bca[(1, NKT - 1)])
            if not to_out:
                for kd in range(8):
                    V(lambda e, kd=kd, src=src_: e.tensor_tensor(
                        rowsb[:, 0:SEQ], src[:, SEQ * kd:SEQ * kd + SEQ],
                        meanb, ALU.subtract),
                      waits=([wlast] if kd == 0 else []))
                    V(lambda e, kd=kd, dst=dst: e.tensor_tensor(
                        dst[:, SEQ * kd:SEQ * kd + SEQ], rowsb[:, 0:SEQ],
                        rstdb, ALU.mult))
            else:
                prev_dma = {}
                prev_oev = {}
                prev_tp = {}
                for tc in range(NQC):
                    otslot = A_ot[:, 1024 * (tc % 2):1024 * (tc % 2) + 1024]
                    oev = None
                    for kd in range(8):
                        w0 = []
                        if kd == 0:
                            w0.append(wlast)
                            if tc - 2 in prev_dma:
                                w0.append(prev_dma[tc - 2])
                        ofa = A_of[:, 256 * (kd % 2):256 * (kd % 2) + 128]
                        ofb = A_of[:, 256 * (kd % 2) + 128:256 * (kd % 2) + 256]
                        if (tc, kd - 2) in prev_tp:
                            w0.append((sems["t"], prev_tp[(tc, kd - 2)]))
                        elif kd < 2 and (tc - 1, kd + 6) in prev_tp:
                            w0.append((sems["t"], prev_tp[(tc - 1, kd + 6)]))
                        V(lambda e, kd=kd, tc=tc, src=src_, ofa=ofa: e.tensor_tensor(
                            ofa, src[:, SEQ * kd + 128 * tc:SEQ * kd + 128 * tc + 128],
                            meanb[:, 128 * tc:128 * tc + 128], ALU.subtract),
                          waits=w0)
                        ofv = V(lambda e, kd=kd, tc=tc, ofa=ofa, ofb=ofb: e.tensor_tensor(
                            ofb, ofa, rstdb[:, 128 * tc:128 * tc + 128], ALU.mult))
                        tpw = [(sems["v"], ofv)]
                        if (tc, kd - 2) in prev_oev:
                            tpw.append((sems["v"], prev_oev[(tc, kd - 2)]))
                        elif kd < 2 and (tc - 1, kd + 6) in prev_oev:
                            tpw.append((sems["v"], prev_oev[(tc - 1, kd + 6)]))
                        tp = T(lambda e, kd=kd, ofb=ofb: e.transpose(
                            ps[:, 3072 + 512 * (kd % 2):3072 + 512 * (kd % 2) + 128],
                            ofb, cidentf), waits=tpw, inc=True)
                        prev_tp[(tc, kd)] = tp
                        oev = V(lambda e, kd=kd, ot=otslot: e.tensor_copy(
                            ot[:, 128 * kd:128 * kd + 128],
                            ps[:, 3072 + 512 * (kd % 2):3072 + 512 * (kd % 2) + 128]),
                            waits=[(sems["t"], tp)])
                        prev_oev[(tc, kd)] = oev
                    prev_dma[tc] = dma_y(out2[s, 128 * tc:128 * tc + 128, :], otslot,
                                         waits=[(sems["v"], oev)])

        emit_ln(x2pre, A_a)
        fence()
        if DUMP[0] and s == 0:
            dma_y(d_x1, A_a)

        if PH[0] < 4 and PH[0] not in (31, 32, 33):
            dma_g(out2[s, 0:128, :], A_a[:, 0:1024])
            break
        x1 = A_a
        x3pre = A_v
        g_rw = dma_g(rowsb[0:8, 0:D], dti["rw"])
        lastr = None
        for kd in range(8):
            lastr = T(lambda e, kd=kd: e.transpose(
                psb[:, 8 * kd:8 * kd + 8],
                rowsb[0:8, 128 * kd:128 * kd + 128], cident[0:8, 0:8]),
                waits=([g_rw] if kd == 0 else []), inc=(kd == 7))
        rwT = A_w[:, 0:64]
        evrw = V(lambda e: e.tensor_copy(rwT, psb[:, 0:64]),
            waits=[(sems["t"], lastr)])
        lastl = None
        for nt in range(NKT):
            for kd in range(8):
                lastl = T(lambda e, nt=nt, kd=kd: e.matmul(
                    ps[32:40, 2048 + 512 * nt:2048 + 512 * nt + 512],
                    rwT[:, 8 * kd:8 * kd + 8],
                    x1[:, SEQ * kd + 512 * nt:SEQ * kd + 512 * nt + 512],
                    start=(kd == 0), stop=(kd == 7), tile_position=(0, 32)),
                    waits=([(sems["v"], evrw)] if kd == 0 and nt == 0 else []),
                    inc=(kd == 7 and nt == NKT - 1))
        evl = V(lambda e: e.tensor_scalar(rowsf[32:40, 0:SEQ],
                                          ps[32:40, 2048:2048 + SEQ],
                                          ebias_c[32:40, 0:1], None, ALU.add),
                waits=[(sems["t"], lastl)])
        prev_ct = {}
        for tc in range(NQC):
            w0 = [(sems["v"], evl)] if tc == 0 else []
            if tc - 2 in prev_ct:
                w0.append((sems["v"], prev_ct[tc - 2]))
            ltc = 2048 + 1024 * (tc % 2)
            lt = T(lambda e, tc=tc, ltc=ltc: e.transpose(
                ps[0:128, ltc:ltc + 8],
                rowsf[32:40, 128 * tc:128 * tc + 128], cidentf[32:40, 32:40]),
                waits=w0, inc=True)
            lg = smallf[:, 0:8]
            m1 = smallf[:, 8:9]
            m2 = smallf[:, 9:10]
            keep = smallf[:, 10:18]
            spv = smallf[:, 18:26]
            ev1 = V(lambda e, ltc=ltc: e.tensor_copy(
                lg, ps[:, ltc:ltc + 8]),
                waits=[(sems["t"], lt), (sems["s"], s2_prev[0])] if s2_prev[0]
                else [(sems["t"], lt)])
            psl = ps[:, ltc:ltc + 8]
            sp_ = lambda e: e.memset(smallf[:, 32:48], 0.0)
            V(sp_)
            V(lambda e, psl=psl: e.reduce_max(m1, psl, axis=AX))
            for _ in range(6):
                V(sp_)
            V(lambda e, psl=psl: e.tensor_scalar(keep, psl, m1, None, ALU.is_ge))
            V(sp_)
            V(sp_)
            V(lambda e, psl=psl: e.scalar_tensor_tensor(
                spv, keep, -1e30, psl, ALU.mult, ALU.add))
            V(sp_)
            V(sp_)
            V(lambda e: e.reduce_max(m2, spv, axis=AX))
            for _ in range(6):
                V(sp_)
            prev_ct[tc] = V(lambda e, psl=psl: e.tensor_scalar(
                keep, psl, m2, None, ALU.is_ge))
            s0 = S(lambda e: e.activation(spv, lg, ACT.Abs),
                   waits=[(sems["v"], sems["v"].n)])
            s1 = S(lambda e: e.activation(spv, spv, ACT.Exp, scale=-1.0))
            s2 = S(lambda e: e.activation(spv, spv, ACT.Ln, bias=1.0))
            s2_prev[0] = s2
            V(lambda e: e.tensor_scalar(lg, lg, 0.0, None, ALU.max),
              waits=[(sems["s"], s2)])
            V(sp_)
            V(lambda e: e.tensor_tensor(lg, lg, spv, ALU.add))
            V(sp_)
            V(lambda e, tc=tc: e.tensor_tensor(
                smallf[:, 64 + 8 * tc:64 + 8 * tc + 8], lg, keep, ALU.mult))
        sq2 = S(lambda e: e.activation(smallf[:, 64:192], smallf[:, 64:192], ACT.Sqrt),
                waits=[(sems["v"], sems["v"].n)])
        prev_cf = {}
        for tc in range(NQC):
            w0 = [(sems["s"], sq2)] if tc == 0 else []
            if tc - 2 in prev_cf:
                w0.append((sems["v"], prev_cf[tc - 2]))
            ctc = 2048 + 1024 * (tc % 2)
            ct = T(lambda e, tc=tc, ctc=ctc: e.transpose(
                ps[0:8, ctc:ctc + 128],
                smallf[:, 64 + 8 * tc:64 + 8 * tc + 8], cidentf),
                waits=w0, inc=True)
            prev_cf[tc] = V(lambda e, tc=tc, ctc=ctc: e.tensor_copy(
                rowsf[0:8, 128 * tc:128 * tc + 128],
                ps[0:8, ctc:ctc + 128]),
                waits=[(sems["t"], ct)])
        cf_last = prev_cf[NQC - 1]
        ms1 = V(lambda e: e.memset(rowsb[0:32, 0:SEQ], 1.0),
                waits=[(sems["v"], cf_last)])
        cvb = V(lambda e: e.tensor_copy(rowsb[0:8, 0:SEQ], rowsf[0:8, 0:SEQ]),
                waits=[(sems["v"], ms1)])
        g_cfd = dma_g(cfdram, rowsf[0:8, 0:SEQ], waits=[(sems["v"], cf_last)])
        biasm = U[:, 12288:13312]
        gb = None
        for e_i in range(8):
            gb = dma_g(biasm[e_i:e_i + 1, 0:D], dti["rb2"][e_i:e_i + 1, :])
        gb = dma_g(biasm[8:9, 0:D], dti["sb2"].rearrange("(o d) -> o d", o=1))
        if DUMP[0] and s == 0:
            dma_y(d_sf, smallf[:, 0:192], waits=[(sems["v"], sems["v"].n)])
            dma_y(d_lg, rowsf[32:40, 0:SEQ], waits=[(sems["v"], evl)])
        fence()

        w1stage = [A_tab[:, 0:4096], U[:, 8192:12288]]
        if PH[0] == 31:
            dma_g(out2[s, 0:128, :], A_a[:, 0:1024])
            break
        yev = {}
        for half in range(2):
            tok0 = 1024 * half
            g_w1 = {}
            g_cb = {}
            sil_done = {}
            scl_done = {}
            lasth = {}
            for e_i in range(NE):
                sl = e_i % 2
                w0 = [precast_done]
                if e_i >= 2:
                    w0.append((sems["t"], lasth[e_i - 2]))
                if w1_free[sl] is not None:
                    w0.append((sems["t"], w1_free[sl]))
                if DUMP[0] and dump_tok[0] is not None:
                    w0.append(dump_tok[0])
                g_w1[e_i] = dma_g(w1stage[sl], w1bf[e_i, :, :], waits=w0)
                if e_i < 8:
                    cb = A_cb[:, 1024 * sl:1024 * sl + 1024]
                    wcb = [g_cfd]
                    if cb_free[sl] is not None:
                        wcb.append((sems["v"], cb_free[sl]))
                    g_cb[e_i] = dma_g(
                        cb, cfdram[e_i:e_i + 1, tok0:tok0 + 1024]
                        .partition_broadcast(128), waits=wcb)
                for sub in range(2):
                    t0 = tok0 + 512 * sub
                    pb = 2048 * sub
                    lh = None
                    for mh in range(4):
                        for kd in range(8):
                            w2_ = []
                            if mh == 0 and kd == 0:
                                w2_.append(g_w1[e_i])
                                if (e_i - 1, sub) in sil_done:
                                    w2_.append((sems["s"], sil_done[(e_i - 1, sub)]))
                                if e_i == 0 and (2 + sub, 3) in yev:
                                    w2_.append((sems["v"], yev[(2 + sub, 3)]))
                            lh = T(lambda e, sl=sl, mh=mh, kd=kd, t0=t0, pb=pb:
                                   e.matmul(
                                       ps[:, pb + 512 * mh:pb + 512 * mh + 512],
                                       w1stage[sl][:, 512 * kd + 128 * mh:512 * kd + 128 * mh + 128],
                                       x1[:, SEQ * kd + t0:SEQ * kd + t0 + 512],
                                       start=(kd == 0), stop=(kd == 7)),
                                   waits=w2_, inc=(kd == 7 and mh == 3))
                    lasth[e_i] = lh
                    w1_free[sl] = lh
                    hv = htil_view(e_i)
                    ssil = None
                    for mh in range(4):
                        ssil = S(lambda e, hv=hv, mh=mh, sub=sub, e_i=e_i, pb=pb:
                                 e.activation(
                                     hv[:, 1024 * mh + 512 * sub:1024 * mh + 512 * sub + 512],
                                     ps[:, pb + 512 * mh:pb + 512 * mh + 512],
                                     ACT.Silu, bias=rb1c[:, 4 * e_i + mh:4 * e_i + mh + 1]),
                                 waits=([(sems["t"], lh)] if mh == 0 else []))
                    sil_done[(e_i, sub)] = ssil
                    if e_i < 8:
                        sc = None
                        for mh in range(4):
                            w3_ = []
                            if mh == 0:
                                w3_ = [(sems["s"], ssil), g_cb[e_i]]
                            sc = V(lambda e, hv=hv, mh=mh, sub=sub, cb=cb: e.tensor_tensor(
                                hv[:, 1024 * mh + 512 * sub:1024 * mh + 512 * sub + 512],
                                hv[:, 1024 * mh + 512 * sub:1024 * mh + 512 * sub + 512],
                                cb[:, 512 * sub:512 * sub + 512], ALU.mult), waits=w3_)
                        scl_done[(e_i, sub)] = sc
                        cb_free[sl] = sc
                    else:
                        scl_done[(e_i, sub)] = V(
                            lambda e: e.memset(smallf[0:1, 190:191], 0.0),
                            waits=[(sems["s"], ssil)])
            if PH[0] == 32:
                continue
            yev = {}
            for sub in range(2):
                t0 = tok0 + 512 * sub
                for grp in range(2):
                    pbase = 2048 * ((2 * sub + grp) % 2)
                    dum = None
                    wd = [(sems["s"], sil_done[(NE - 1, 1)]),
                          (sems["v"], scl_done[(7, 1)])]
                    if (sub, grp) != (0, 0):
                        pass
                    for mdl in range(4):
                        if YMODE[0] == 2 and mdl > 0:
                            continue
                        w5_ = list(wd) if mdl == 0 else []
                        if mdl == 0 and ((2 * sub + grp) - 2, 3) in yev:
                            w5_.append((sems["v"], yev[((2 * sub + grp) - 2, 3)]))
                        if YMODE[0] == 2:
                            for mdl2 in range(4):
                                if ((2 * sub + grp) - 2, mdl2) in yev:
                                    w5_.append((sems["v"], yev[((2 * sub + grp) - 2, mdl2)]))
                            dum = T(lambda e: e.nop(), waits=w5_, inc=True)
                            continue
                        dum = T(lambda e, pbase=pbase, mdl=mdl: e.matmul(
                            ps[:, pbase + 512 * mdl:pbase + 512 * mdl + 512],
                            zrow[0:1, 0:128], zrow, start=True, stop=False),
                            waits=w5_, inc=(mdl == 3))
                    lasty = {}
                    for e_i in range(NE):
                        if YONLY[0] and e_i > 0:
                            lasty[e_i] = lasty[e_i - 1]
                            continue
                        sl2 = e_i % 2
                        w7_ = [precast_done]
                        if e_i >= 2:
                            w7_.append((sems["t"], lasty[e_i - 2]))
                        if w2_free[sl2] is not None:
                            w7_.append((sems["t"], w2_free[sl2]))
                        gw2 = dma_g(U_w2db[sl2].rearrange("p (kh c) -> p kh c", kh=4),
                                    w2bf[e_i, :, :].rearrange(
                                        "p (kh c) -> p kh c", kh=4)[:, :, 512 * grp:512 * grp + 512],
                                    waits=w7_)
                        ly = None
                        for mdl in range(4):
                            md = 4 * grp + mdl
                            hv = htil_view(e_i)
                            for kh in range(4):
                                w4_ = []
                                if mdl == 0 and kh == 0:
                                    w4_.append(gw2)
                                    if e_i == 0:
                                        w4_.append((sems["t"], dum))
                                st_ = (YMODE[0] == 2 and e_i == 0 and kh == 0)
                                sp_ = (YMODE[0] == 2 and e_i == 0 and kh == 3)
                                ly = T(lambda e, sl2=sl2, mdl=mdl, kh=kh, hv=hv,
                                       sub=sub, pbase=pbase, st_=st_, sp_=sp_: e.matmul(
                                           ps[:, pbase + 512 * mdl:pbase + 512 * mdl + 512],
                                           U_w2db[sl2][:, 512 * kh + 128 * mdl:512 * kh + 128 * mdl + 128],
                                           hv[:, 1024 * kh + 512 * sub:1024 * kh + 512 * sub + 512],
                                           start=st_, stop=sp_),
                                       waits=w4_, inc=(mdl == 3 and kh == 3))
                        lasty[e_i] = ly
                        w2_free[sl2] = ly
                    lb = None
                    for mdl in range(4):
                        if YMODE[0] == 2:
                            lb = lasty[NE - 1]
                            break
                        md = 4 * grp + mdl
                        w6_ = [gb, (sems["v"], cvb)] if mdl == 0 else []
                        lb = T(lambda e, md=md, mdl=mdl, t0=t0, pbase=pbase: e.matmul(
                            ps[:, pbase + 512 * mdl:pbase + 512 * mdl + 512],
                            biasm[0:9, 128 * md:128 * md + 128],
                            rowsb[0:9, t0:t0 + 512], start=False, stop=True),
                            waits=w6_, inc=(mdl == 3))
                    if DUMP[0] and s == 0 and half == 0 and sub == 0 and grp == 0:
                        vv = V(lambda e: e.tensor_copy(rowsf[:, 0:2048].bitcast(F32) if False else rowsf[:, 0:2048], ps[:, 0:2048]),
                               waits=[(sems["t"], lb)])
                        dma_y(d_y, rowsf[:, 0:2048], waits=[(sems["v"], vv)])
                    for mdl in range(4):
                        md = 4 * grp + mdl
                        yev[((2 * sub + grp), mdl)] = V(
                            lambda e, md=md, mdl=mdl, t0=t0, pbase=pbase:
                            e.scalar_tensor_tensor(
                                x3pre[:, SEQ * md + t0:SEQ * md + t0 + 512],
                                ps[:, pbase + 512 * mdl:pbase + 512 * mdl + 512], 1.0,
                                x1[:, SEQ * md + t0:SEQ * md + t0 + 512],
                                ALU.mult, ALU.add),
                            waits=[(sems["t"], lb)])
            if DUMP[0] and s == 0 and half == 0:
                dump_tok[0] = dma_y(d_ht0, A_q[:, 0:4096],
                                    waits=[(sems["t"], sems["t"].n)])
        fence()

        if DUMP[0] and s == 0:
            dma_y(d_x3pre, A_v)
            dma_y(d_cf, cfdram)
            dma_y(d_ht, A_q[:, 0:4096])
            dma_y(d_w2, U[:, 4096:8192])
        if PH[0] < 5:
            dma_g(out2[s, 0:128, :], A_v[:, 0:1024])
            break
        emit_ln(x3pre, None, to_out=True)
        fence()

    dfin = [(c, c.n) for k in "yg" for c in dpool[k]]
    def _fin(eng, dfin=dfin):
        for (c, v) in dfin:
            if v > 0:
                eng.wait_ge(c.sem, v)
    plan["y"].append(_fin)

    with nc.Block() as blk:
        @blk.tensor
        def _(eng):
            for f in plan["t"]:
                f(eng)
        @blk.vector
        def _(eng):
            for f in plan["v"]:
                f(eng)
        @blk.scalar
        def _(eng):
            for f in plan["s"]:
                f(eng)
        @blk.sync
        def _(eng):
            for f in plan["y"]:
                f(eng)
        @blk.gpsimd
        def _(eng):
            for f in plan["g"]:
                f(eng)
    return nc


s2_prev = [None]


def kernel(L0_attn_x, L0_attn_qkv_w, L0_attn_proj_w, L0_attn_proj_b, L0_attn_slopes,
           L0_ffn_routing_W, L0_ffn_expert_bias, L0_ffn_sW1, L0_ffn_sb1, L0_ffn_sW2,
           L0_ffn_sb2, L0_ffn_rW1, L0_ffn_rb1, L0_ffn_rW2, L0_ffn_rb2):
    s2_prev[0] = None
    x = np.ascontiguousarray(np.asarray(L0_attn_x, dtype=np.float32))
    consts = _host_consts()
    nc = build_program()
    common = {
        "qkvw": np.ascontiguousarray(np.asarray(L0_attn_qkv_w, np.float32)),
        "projw": np.ascontiguousarray(np.asarray(L0_attn_proj_w, np.float32)),
        "projb": np.ascontiguousarray(np.asarray(L0_attn_proj_b, np.float32)),
        "slopes": np.ascontiguousarray(np.asarray(L0_attn_slopes, np.float32)),
        "rw": np.ascontiguousarray(np.asarray(L0_ffn_routing_W, np.float32)),
        "ebias": np.ascontiguousarray(np.asarray(L0_ffn_expert_bias, np.float32)),
        "sw1": np.ascontiguousarray(np.asarray(L0_ffn_sW1, np.float32)),
        "sb1": np.ascontiguousarray(np.asarray(L0_ffn_sb1, np.float32)),
        "sw2": np.ascontiguousarray(np.asarray(L0_ffn_sW2, np.float32)),
        "sb2": np.ascontiguousarray(np.asarray(L0_ffn_sb2, np.float32)),
        "rw1": np.ascontiguousarray(np.asarray(L0_ffn_rW1, np.float32)),
        "rb1": np.ascontiguousarray(np.asarray(L0_ffn_rb1, np.float32)),
        "rw2": np.ascontiguousarray(np.asarray(L0_ffn_rW2, np.float32)),
        "rb2": np.ascontiguousarray(np.asarray(L0_ffn_rb2, np.float32)),
    }
    common.update(consts)
    in_maps = []
    for c in range(NCORES):
        m = dict(common)
        m["x"] = np.ascontiguousarray(x[2 * c:2 * c + 2].reshape(NSEQ * SEQ, D))
        in_maps.append(m)
    res = run_bass_kernel_spmd(nc, in_maps, core_ids=list(range(NCORES)))
    outs = [np.asarray(res.results[c]["out"]).reshape(NSEQ, SEQ, D)
            for c in range(NCORES)]
    return np.concatenate(outs, axis=0).astype(np.float32)


# version 09 (final)
# speedup vs baseline: 1.1849x; 1.1849x over previous
import numpy as np
import ml_dtypes
import concourse.bass as bass
import concourse.mybir as mybir
from concourse.bass_utils import run_bass_kernel_spmd

F32 = mybir.dt.float32
BF16 = mybir.dt.bfloat16
ALU = mybir.AluOpType
ACT = mybir.ActivationFunctionType
AX = mybir.AxisListType.X

NCORES = 8
SEQ = 2048
NSEQ = 2
D = 1024
H = 16
NP = 8
NQC = 16
NKT = 4
EW = 2176
HF = 512
NE = 9


def _bf(x):
    return np.ascontiguousarray(np.asarray(x, dtype=ml_dtypes.bfloat16))


def _host_consts():
    c = {}
    c["cident"] = _bf(np.eye(128))
    c["cidentf"] = np.ascontiguousarray(np.eye(128, dtype=np.float32))
    R = np.zeros((128, 128))
    for blk in range(2):
        b = 64 * blk
        for m in range(32):
            R[b + m + 32, b + m] = -1.0
        for m in range(32, 64):
            R[b + m - 32, b + m] = 1.0
    c["crot"] = _bf(R)
    d_idx = np.arange(128) % 64
    inv = 1.0 / (10000.0 ** (np.arange(0, 64, 2, dtype=np.float64) / 64.0))
    fr = inv[d_idx % 32]
    ang = fr[:, None] * np.arange(SEQ, dtype=np.float64)[None, :]
    c["ccos"] = _bf(np.cos(ang))
    c["csin"] = _bf(np.sin(ang))
    p = np.arange(128)[:, None]
    col = np.arange(EW)[None, :]
    c["cminm"] = np.ascontiguousarray(
        np.minimum(col - p - 2048, 0).astype(np.float32))
    return c


DBG = {}
DUMP = [False]
YONLY = [False]
YMODE = [0]
PH = [5]
QCX = [16]


def build_program():
    nc = bass.Bass("TRN2", target_bir_lowering=False, detect_race_conditions=False)

    dti = {}
    def din(name, shape, dtype=F32):
        dti[name] = nc.dram_tensor(name, list(shape), dtype, kind="ExternalInput").ap()
    din("x", [NSEQ * SEQ, D])
    din("qkvw", [3 * D, D]); din("projw", [D, D]); din("projb", [D])
    din("slopes", [H])
    din("rw", [8, D]); din("ebias", [8])
    din("sw1", [D, HF]); din("sb1", [HF]); din("sw2", [HF, D]); din("sb2", [D])
    din("rw1", [8, D, HF]); din("rb1", [8, HF])
    din("rw2", [8, HF, D]); din("rb2", [8, D])
    din("cident", [128, 128], BF16); din("cidentf", [128, 128], F32)
    din("crot", [128, 128], BF16)
    din("ccos", [128, SEQ], BF16); din("csin", [128, SEQ], BF16)
    din("cminm", [128, EW], F32)
    out_d = nc.dram_tensor("out", [NSEQ * SEQ, D], F32, kind="ExternalOutput").ap()
    w1bf = nc.dram_tensor("w1bf", [NE, 128, 4096], BF16, kind="Internal").ap()
    w2bf = nc.dram_tensor("w2bf", [NE, 128, 4096], BF16, kind="Internal").ap()
    cfdram = nc.dram_tensor("cfdram", [8, SEQ], F32, kind="Internal").ap()
    if DUMP[0]:
        d_attn = nc.dram_tensor("d_attn", [128, NP * SEQ], BF16, kind="ExternalOutput").ap()
        d_x2pre = nc.dram_tensor("d_x2pre", [128, NP * SEQ], BF16, kind="ExternalOutput").ap()
        d_x1 = nc.dram_tensor("d_x1", [128, NP * SEQ], BF16, kind="ExternalOutput").ap()
        d_x3pre = nc.dram_tensor("d_x3pre", [128, NP * SEQ], BF16, kind="ExternalOutput").ap()
        d_cf = nc.dram_tensor("d_cf", [8, SEQ], F32, kind="ExternalOutput").ap()
        d_ht = nc.dram_tensor("d_ht", [128, 4096], BF16, kind="ExternalOutput").ap()
        d_w2 = nc.dram_tensor("d_w2", [128, 4096], BF16, kind="ExternalOutput").ap()
        d_y = nc.dram_tensor("d_y", [128, 2048], F32, kind="ExternalOutput").ap()
        d_ht0 = nc.dram_tensor("d_ht0", [128, 4096], BF16, kind="ExternalOutput").ap()
        d_ltmp = nc.dram_tensor("d_ltmp", [128, 64], F32, kind="ExternalOutput").ap()
        d_sf = nc.dram_tensor("d_sf", [128, 192], F32, kind="ExternalOutput").ap()
        d_lg = nc.dram_tensor("d_lg", [8, 2048], F32, kind="ExternalOutput").ap()
        d_ud = nc.dram_tensor("d_ud", [128, 512], BF16, kind="ExternalOutput").ap()
        d_up = nc.dram_tensor("d_up", [128, 8192], BF16, kind="ExternalOutput").ap()
        dump_tok = [None]

    plan = {k: [] for k in "tvsyg"}
    sems = {k: None for k in "tvsyg"}
    class Ctr:
        def __init__(self, name):
            self.sem = nc.alloc_semaphore(name)
            self.n = 0
    for k in "tvsyg":
        sems[k] = Ctr("sem_" + k)

    def _emit(key, fn, waits, inc, inck):
        for (c, v) in waits:
            assert v is not None, "wait on None count (missing inc=True?)"
        w = tuple((c, v) for (c, v) in waits if v and v > 0)
        ctr = sems[key]
        cnt = None
        if inc:
            ctr.n += inck
            cnt = ctr.n
        def run(eng, fn=fn, w=w, cnt=cnt, ctr=ctr, inck=inck):
            for (c, v) in w:
                eng.wait_ge(c.sem, v)
            ins = fn(eng)
            if cnt is not None:
                ins.then_inc(ctr.sem, inck)
        plan[key].append(run)
        return cnt

    def T(fn, waits=(), inc=False):
        return _emit("t", fn, waits, inc, 1)
    def V(fn, waits=(), inc=True):
        return _emit("v", fn, waits, inc, 1)
    def S(fn, waits=(), inc=True):
        return _emit("s", fn, waits, inc, 1)

    NDMA = 8
    dpool = {k: [Ctr(f"dma_{k}{i}") for i in range(NDMA)] for k in "yg"}
    didx = {"y": 0, "g": 0}

    def _dma(key, fn, waits):
        c = dpool[key][didx[key] % NDMA]
        didx[key] += 1
        for (cc, v) in waits:
            assert v is not None, "dma wait on None count (missing inc=True?)"
        w = tuple((cc, v) for (cc, v) in waits if v and v > 0)
        pre = c.n
        c.n += 16
        tok = (c, c.n)
        def run(eng, fn=fn, w=w, c=c, pre=pre):
            if pre > 0:
                eng.wait_ge(c.sem, pre)
            for (cc, v) in w:
                eng.wait_ge(cc.sem, v)
            fn(eng).then_inc(c.sem, 16)
        plan[key].append(run)
        return tok

    def Y(fn, waits=()):
        return _dma("y", fn, waits)
    def G(fn, waits=()):
        return _dma("g", fn, waits)

    def fence():
        snap = {k: sems[k].n for k in "tvsyg"}
        dsnap = [(c, c.n) for k in "yg" for c in dpool[k]]
        for k in "tvsyg":
            def run(eng, k=k, snap=snap, dsnap=dsnap):
                for j in "tvsyg":
                    if j != k and snap[j] > 0:
                        eng.wait_ge(sems[j].sem, snap[j])
                for (c, v) in dsnap:
                    if v > 0:
                        eng.wait_ge(c.sem, v)
            plan[k].append(run)

    al = nc.alloc_sbuf_tensor
    cident = al("sb_cident", [128, 128], BF16).ap()
    cidentf = al("sb_cidentf", [128, 128], F32).ap()
    crot = al("sb_crot", [128, 128], BF16).ap()
    onescol = al("sb_onescol", [128, 128], F32).ap()
    zrow = al("sb_zrow", [1, 512], BF16).ap()
    ones128 = al("sb_ones128", [128, 1], BF16).ap()
    s8col = al("sb_s8col", [128, H], F32).ap()
    projbc = al("sb_projbc", [128, 8], F32).ap()
    rb1c = al("sb_rb1c", [128, 4 * NE], F32).ap()
    ebias_c = al("sb_ebias_c", [128, 1], F32).ap()
    rowsf = al("sb_rowsf", [128, SEQ], F32).ap()
    rowsb = al("sb_rowsb", [128, SEQ], BF16).ap()
    ltmp = al("sb_ltmp", [128, 64], F32).ap()
    smallf = al("sb_smallf", [128, 192], F32).ap()

    A_q = al("sb_A_q", [128, NP * SEQ], BF16).ap()
    A_k = al("sb_A_k", [128, NP * SEQ], BF16).ap()
    A_v = al("sb_A_v", [128, NP * SEQ], BF16).ap()
    A_a = al("sb_A_a", [128, NP * SEQ], BF16).ap()
    U = al("sb_U", [128, 17408], BF16).ap()
    A_tab = al("sb_A_tab", [128, 2 * SEQ], BF16).ap()
    A_w = al("sb_A_w", [128, 2 * D], BF16).ap()
    A_w2 = al("sb_A_w2", [128, 2 * D], BF16).ap()
    A_bc = al("sb_A_bc", [128, 2 * SEQ], BF16).ap()
    A_cb = A_bc[:, 0:2048]
    A_qf = A_q.bitcast(F32)
    A_ot = A_qf[:, 0:2048]
    A_of = A_qf[:, 2048:2560]

    assert nc.sbuf_bytes_remaining >= 0
    DBG.update(ltmp=ltmp, A_q=A_q, A_k=A_k,
               A_v=A_v, A_a=A_a, U=U, rowsf=rowsf, smallf=smallf,
               s8col=s8col)

    ps = nc.alloc_psum_tensor("ps", [128, 4096], F32).ap()
    psb = ps.bitcast(BF16)
    DBG['ps'] = ps

    def dma_y(dst, src, waits=()):
        return Y(lambda e, d=dst, s=src: e.dma_start(d, s), waits=waits)
    def dma_g(dst, src, waits=()):
        return G(lambda e, d=dst, s=src: e.dma_start(d, s), waits=waits)

    dma_y(cident, dti["cident"])
    dma_y(cidentf, dti["cidentf"])
    dma_y(crot, dti["crot"])
    Y(lambda e: e.dma_start(projbc, dti["projb"].rearrange("(m p) -> p m", p=128),
                            allow_slow_non_contiguous=True))
    dma_y(ebias_c[32:40, 0:1], dti["ebias"].rearrange("(e o) -> e o", o=1))
    y_sl = dma_y(smallf[0:1, 176:192], dti["slopes"].rearrange("(o h) -> o h", o=1))
    mm_ones = V(lambda e: e.memset(onescol, 1.0))
    t_sl = T(lambda e: e.matmul(ps[:, 0:16], onescol[0:1, 0:128],
                                smallf[0:1, 176:192], start=True, stop=True),
             waits=[y_sl, (sems["v"], mm_ones)], inc=True)
    V(lambda e: e.tensor_scalar_mul(s8col, ps[:, 0:16], 8.0),
      waits=[(sems["t"], t_sl)])
    for e_i in range(8):
        Y(lambda e, e_i=e_i: e.dma_start(
            rb1c[:, 4 * e_i:4 * e_i + 4],
            dti["rb1"][e_i, :].rearrange("(m p) -> p m", p=128),
            allow_slow_non_contiguous=True))
    Y(lambda e: e.dma_start(rb1c[:, 32:36],
                            dti["sb1"].rearrange("(m p) -> p m", p=128),
                            allow_slow_non_contiguous=True))
    V(lambda e: e.memset(zrow, 0.0))
    V(lambda e: e.memset(ones128, 1.0))
    V(lambda e: e.memset(ltmp[:], 0.0))

    pre_done = []
    for kind in (1, 2):
        for e_i in range(NE):
            sl = e_i % 2
            stg = A_q[:, 8192 * sl:8192 * sl + 4096]
            if kind == 1:
                src = (dti["rw1"][e_i, :, :] if e_i < 8 else dti["sw1"])
                srcv = src.rearrange("(kd p) c -> p kd c", p=128)
                dstd = w1bf[e_i, :, :]
            else:
                src = (dti["rw2"][e_i, :, :] if e_i < 8 else dti["sw2"])
                srcv = src.rearrange("(kh p) c -> p kh c", p=128)
                dstd = w2bf[e_i, :, :]
            w0 = [pre_done[-1]] if len(pre_done) >= 1 else []
            g1 = dma_g(stg.rearrange("p (a c) -> p a c", a=(8 if kind == 1 else 4)),
                       srcv, waits=w0)
            g2 = dma_g(dstd, stg, waits=[g1])
            pre_done.append(g2)
    precast_done = pre_done[-1]
    fence()

    out2 = out_d.rearrange("(s t) d -> s t d", s=NSEQ)
    x2 = dti["x"].rearrange("(s t) d -> s t d", s=NSEQ)

    tab_cos = A_tab[:, 0:SEQ]
    tab_sin = A_tab[:, SEQ:2 * SEQ]

    U_P = [U[:, SEQ * i:SEQ * i + SEQ] for i in range(4)]
    U_PT = [U[:, 8192 + SEQ * i:8192 + SEQ * i + SEQ] for i in range(2)]
    U_D = [U[:, 12288 + 128 * i:12288 + 128 * i + 128] for i in range(4)]
    U_x = U[:, 0:NP * SEQ]
    U_h8 = U[:, 0:4096]
    U_w2db = [U[:, 4096:4096 + 2048], U[:, 6144:6144 + 2048]]

    def htil_view(e_i):
        if e_i < 4:
            return A_q[:, 4096 * e_i:4096 * e_i + 4096]
        if e_i < 8:
            return A_k[:, 4096 * (e_i - 4):4096 * (e_i - 4) + 4096]
        return U_h8

    w1_free = {0: None, 1: None}
    cb_free = {0: None, 1: None}
    w2_free = {0: None, 1: None}

    nseq_run = NSEQ if PH[0] >= 5 else 1
    for s in range(nseq_run):
        w_tab = None
        for tsl, tdram in ((tab_cos, "ccos"), (tab_sin, "csin")):
            w_tab = dma_y(tsl, dti[tdram])

        def load_xfm():
            tload = {}
            vev = {}
            for tc in range(NQC):
                slot = A_w[:, D * (tc % 2):D * (tc % 2) + D]
                w0 = [(sems["t"], tload[tc - 2])] if tc >= 2 else []
                g_ld = dma_g(slot, x2[s, 128 * tc:128 * tc + 128, :], waits=w0)
                last = None
                for kd in range(8):
                    w1_ = []
                    if kd == 0:
                        w1_.append(g_ld)
                        if tc >= 1:
                            w1_.append((sems["v"], vev[tc - 1]))
                    last = T(lambda e, slot=slot, kd=kd: e.transpose(
                        psb[:, 128 * kd:128 * kd + 128],
                        slot[:, 128 * kd:128 * kd + 128], cident),
                        waits=w1_, inc=(kd == 7))
                tload[tc] = last
                dstv = U_x.rearrange("p (kd t) -> p kd t", kd=8)[:, :, 128 * tc:128 * tc + 128]
                srcv = psb[:, 0:1024].rearrange("p (kd t) -> p kd t", kd=8)
                vev[tc] = V(lambda e, o=dstv, i=srcv: e.tensor_copy(o, i),
                            waits=[(sems["t"], last)])
            return vev[NQC - 1]

        x_ready = load_xfm()
        fence()

        wnat = [A_w[:, 0:D], A_w[:, D:2 * D]]
        wT = [A_w2[:, 0:D], A_w2[:, D:2 * D]]
        wT_ev = {}
        wT_last = {}
        mm_last = {}
        bank_ev = {}
        prev_rot = None
        prev_t2 = None
        prev_vev = None
        prev_nm = None
        prev_ntp = None
        prev_vtp = None
        for mo in range(24):
            sl = mo % 2
            w0 = [(sems["t"], wT_last[mo - 2])] if mo >= 2 else []
            g_w = dma_g(wnat[sl], dti["qkvw"][128 * mo:128 * mo + 128, :], waits=w0)
            lw = None
            for kd in range(8):
                w1_ = []
                if kd == 0:
                    w1_.append(g_w)
                    if mo >= 2:
                        w1_.append((sems["v"], wT_ev[mo - 2]))
                lw = T(lambda e, sl=sl, kd=kd: e.transpose(
                    psb[:, 3072 + 128 * kd:3072 + 128 * kd + 128],
                    wnat[sl][:, 128 * kd:128 * kd + 128], cident),
                    waits=w1_, inc=(kd == 7))
            wev = [(sems["t"], lw)]
            if mo - 2 in mm_last:
                wev.append((sems["t"], mm_last[mo - 2]))
            if mo < 8:
                ev1 = V(lambda e, sl=sl: e.tensor_scalar_mul(
                    wT[sl], psb[:, 3072:4096], 0.125), waits=wev)
            else:
                ev1 = V(lambda e, sl=sl: e.tensor_copy(
                    wT[sl], psb[:, 3072:4096]), waits=wev)
            wT_ev[mo] = ev1
            wT_last[mo] = lw

            p_i = mo % 8
            kind = "q" if mo < 8 else ("k" if mo < 16 else "v")
            store = A_q if kind == "q" else (A_k if kind == "k" else None)
            for nt in range(NKT):
                bank = 512 * (nt % 2)
                w0 = [(sems["v"], ev1)]
                if (nt % 2) in bank_ev:
                    w0.append((sems["v"], bank_ev[nt % 2]))
                if mo == 0 and nt == 0:
                    w0.append((sems["v"], x_ready))
                last = None
                for kd in range(8):
                    last = T(lambda e, bank=bank, kd=kd, sl=sl, nt=nt: e.matmul(
                        ps[:, bank:bank + 512],
                        wT[sl][:, 128 * kd:128 * kd + 128],
                        U_x[:, SEQ * kd + 512 * nt:SEQ * kd + 512 * nt + 512],
                        start=(kd == 0), stop=(kd == 7)),
                        waits=(w0 if kd == 0 else []), inc=(kd == 7))

                mm_last[mo] = last
                if kind in ("q", "k"):
                    qraw = rowsb[:, 0:512]
                    t1b = rowsb[:, 512:1024]
                    t2b = rowsb[:, 1024:1536]
                    sqb = rowsb[:, 1536:2048]
                    wv = [(sems["t"], last)]
                    if prev_rot is not None:
                        wv.append((sems["t"], prev_rot))
                    cpy = V(lambda e, bank=bank, q=qraw: e.tensor_copy(
                        q, ps[:, bank:bank + 512]), waits=wv)
                    bank_ev[nt % 2] = cpy
                    wrot = [(sems["v"], cpy)]
                    if prev_t2 is not None:
                        wrot.append((sems["v"], prev_t2))
                    rot = T(lambda e, q=qraw: e.matmul(
                        ps[:, 1024:1536], crot, q, start=True, stop=True),
                        waits=wrot, inc=True)
                    prev_rot = rot
                    ctab = tab_cos
                    stab = tab_sin
                    V(lambda e, q=qraw, ct=ctab, nt=nt, t1=t1b: e.tensor_tensor(
                        t1, q, ct[:, 512 * nt:512 * nt + 512], ALU.mult),
                      waits=[w_tab])
                    prev_t2 = V(lambda e, st=stab, nt=nt, t2=t2b: e.scalar_tensor_tensor(
                        t2, ps[:, 1024:1536], 1.0, st[:, 512 * nt:512 * nt + 512],
                        ALU.mult, ALU.mult),
                      waits=[(sems["t"], rot)])
                    dst = store[:, SEQ * p_i + 512 * nt:SEQ * p_i + 512 * nt + 512]
                    vq_ev_dst = V(lambda e, d=dst, t1=t1b, t2=t2b: e.tensor_tensor(
                        d, t1, t2, ALU.add))
                else:
                    vraw = rowsb[:, 0:512]
                    wv = [(sems["t"], last)]
                    if prev_vtp is not None:
                        wv.append((sems["t"], prev_vtp))
                    cpy = V(lambda e, bank=bank, q=vraw: e.tensor_copy(
                        q, ps[:, bank:bank + 512]), waits=wv)
                    bank_ev[nt % 2] = cpy
                    vtp = None
                    for i4 in range(4):
                        wv2 = []
                        if i4 == 0:
                            wv2.append((sems["v"], cpy))
                            if prev_vev is not None:
                                wv2.append((sems["v"], prev_vev))
                        vtp = T(lambda e, i4=i4: e.transpose(
                            psb[:, 4096 + 128 * i4:4096 + 128 * i4 + 128],
                            vraw[:, 128 * i4:128 * i4 + 128], cident),
                            waits=wv2, inc=(i4 == 3))
                    prev_vtp = vtp
                    dstv = A_v.rearrange("p (tc d) -> p tc d", tc=NQC)[
                        :, 4 * nt:4 * nt + 4, 128 * p_i:128 * p_i + 128]
                    srcv = psb[:, 4096:4608].rearrange("p (i4 d) -> p i4 d", i4=4)
                    prev_vev = V(lambda e, o=dstv, i=srcv: e.tensor_copy(o, i),
                      waits=[(sems["t"], vtp)])
        fence()

        skip_attn = PH[0] < 2
        g_mm1 = dma_y(rowsf, dti["cminm"][:, 0:2048])
        g_mm2 = dma_y(smallf[:, 0:128], dti["cminm"][:, 2048:2176])
        prev_exp2 = None
        prev_bm = [None]
        pt_bank_free = {2560: None, 3584: None}
        last_pve = None
        pv_last = None
        for p_i in range(NP if not skip_attn else 0):
            qk_inc = {}
            exp_inc = {}
            vD_inc = {}
            pt_inc = {}
            ptev_inc = {}
            pv_inc = {}
            pve_inc = {}

            def emit_qk(j, qc):
                base = 64 * j
                stat = A_q[base:base + 64, SEQ * p_i + 128 * qc:SEQ * p_i + 128 * qc + 128]
                w0 = [(sems["v"], prev_bm[0])] if prev_bm[0] else []
                if p_i == 0 and qc == 0 and j == 0:
                    w0.append(g_mm1)
                    w0.append(g_mm2)
                last = None
                for kt in range(NKT):
                    mov = A_k[base:base + 64, SEQ * p_i + 512 * kt:SEQ * p_i + 512 * kt + 512]
                    last = T(lambda e, stat=stat, mov=mov, kt=kt: e.matmul(
                        ps[:, 512 * kt:512 * kt + 512], stat, mov,
                        start=True, stop=True),
                        waits=(w0 if kt == 0 else []), inc=(kt == 3))
                qk_inc[(j, qc)] = last

            def emit_bias_max(j, qc):
                h = 2 * p_i + j
                par = qc % 2
                pbuf = U_P[2 * j + par]
                s8 = s8col[:, h:h + 1]
                w0 = [(sems["t"], qk_inc[(j, qc)])]
                if (j, qc - 2, 3) in pt_inc:
                    w0.append((sems["t"], pt_inc[(j, qc - 2, 3)]))
                if (j, qc - 2) in vD_inc:
                    w0.append((sems["v"], vD_inc[(j, qc - 2)]))
                wp = 128 * qc
                if qc > 0:
                    V(lambda e, pb=pbuf, w=wp, s8=s8: e.scalar_tensor_tensor(
                        pb[:, 0:w], rowsf[:, 2048 - w:2048], s8, ps[:, 0:w],
                        ALU.mult, ALU.add), waits=w0)
                    w0 = []
                V(lambda e, pb=pbuf, w=wp, s8=s8: e.scalar_tensor_tensor(
                    pb[:, w:w + 128], smallf[:, 0:128], s8, ps[:, w:w + 128],
                    ALU.mult, ALU.add), waits=w0)
                if wp + 128 < SEQ:
                    V(lambda e, pb=pbuf, w=wp: e.tensor_copy(
                        pb[:, w + 128:SEQ], ps[:, w + 128:SEQ]))
                bm_inc[(j, qc)] = sems["v"].n
                prev_bm[0] = bm_inc[(j, qc)]

            def emit_exp(j, qc):
                nonlocal prev_exp2
                par = qc % 2
                pbuf = U_P[2 * j + par]
                l0 = 32 * par + 16 * j
                wp = [(sems["v"], bm_inc[(j, qc)])]
                s_l1 = S(lambda e, pb=pbuf, l0=l0: e.activation(
                    rowsb[:, 0:SEQ], pb[:, 0:SEQ], ACT.Exp, scale=0.25,
                    accum_out=ltmp[:, l0:l0 + 1]), waits=wp)
                V(lambda e, l0=l0: e.tensor_tensor(
                    ltmp[:, l0 + 2:l0 + 3], ltmp[:, l0:l0 + 1],
                    ltmp[:, l0 + 1:l0 + 2], ALU.add),
                  waits=[(sems["s"], s_l1)])
                v_l1 = sems["v"].n
                sln = S(lambda e, l0=l0: e.activation(
                    ltmp[:, l0 + 3:l0 + 4], ltmp[:, l0 + 2:l0 + 3], ACT.Ln),
                    waits=[(sems["v"], v_l1)])
                V(lambda e, l0=l0: e.tensor_scalar(
                    ltmp[:, l0 + 3:l0 + 4], ltmp[:, l0 + 3:l0 + 4],
                    -4.0, None, ALU.mult), waits=[(sems["s"], sln)])
                v_negm = sems["v"].n
                negm = ltmp[:, l0 + 3:l0 + 4]
                e2 = S(lambda e, pb=pbuf, b=negm, l0=l0:
                    e.activation(pb[:, 0:SEQ], pb[:, 0:SEQ], ACT.Exp, bias=b,
                                 accum_out=ltmp[:, l0 + 4:l0 + 5]),
                    waits=[(sems["v"], v_negm)])
                exp_inc[(j, qc, 1)] = e2
                exp_inc[(j, qc, 2)] = e2
                prev_exp2 = e2

            def emit_l_D(j, qc):
                par = qc % 2
                l0 = 32 * par + 16 * j
                V(lambda e, l0=l0: e.tensor_tensor(
                    ltmp[:, l0 + 6:l0 + 7], ltmp[:, l0 + 4:l0 + 5],
                    ltmp[:, l0 + 5:l0 + 6], ALU.add),
                  waits=[(sems["s"], exp_inc[(j, qc, 2)])])
                for _ in range(4):
                    V(lambda e, l0=l0: e.memset(ltmp[:, l0 + 9:l0 + 10], 0.0))
                V(lambda e, l0=l0: e.reciprocal(
                    ltmp[:, l0 + 8:l0 + 9], ltmp[:, l0 + 6:l0 + 7]))
                for _ in range(4):
                    V(lambda e, l0=l0: e.memset(ltmp[:, l0 + 9:l0 + 10], 0.0))
                dbuf = U_D[2 * j + par]
                wd = []
                if (j, qc - 2, 3) in pt_inc:
                    wd.append((sems["t"], pt_inc[(j, qc - 2, 3)]))
                vD_inc[(j, qc)] = V(lambda e, d=dbuf, l0=l0: e.tensor_scalar(
                    d, cident, ltmp[:, l0 + 8:l0 + 9], None, ALU.mult), waits=wd)

            def emit_pt(j, qc):
                par = qc % 2
                pbuf = U_P[2 * j + par]
                dbuf = U_D[2 * j + par]
                for b in range(4):
                    bank = 2560 if b % 2 == 0 else 3584
                    w0 = []
                    if b == 0:
                        w0.append((sems["v"], vD_inc[(j, qc)]))
                        if qc >= 1 and (qc - 1) in pv_inc:
                            w0.append((sems["t"], pv_inc[qc - 1]))
                    if b >= 2:
                        w0.append((sems["v"], ptev_inc[(j, qc, b - 2)]))
                    elif pt_bank_free[bank] is not None:
                        w0.append((sems["v"], pt_bank_free[bank]))
                    last = None
                    for i4 in range(4):
                        kc = 4 * b + i4
                        last = T(lambda e, pb=pbuf, d=dbuf, kc=kc, bank=bank, i4=i4:
                                 e.matmul(ps[:, bank + 128 * i4:bank + 128 * i4 + 128],
                                          pb[:, 128 * kc:128 * kc + 128], d,
                                          start=True, stop=True),
                                 waits=(w0 if i4 == 0 else []), inc=(i4 == 3))
                    pt_inc[(j, qc, b)] = last
                    dst = U_PT[j][:, 512 * b:512 * b + 512]
                    ev = V(lambda e, o=dst, bank=bank: e.tensor_copy(
                        o, ps[:, bank:bank + 512]),
                        waits=[(sems["t"], last)])
                    ptev_inc[(j, qc, b)] = ev
                    pt_bank_free[bank] = ev

            def emit_pv(qc):
                slot = 128 * (qc % 4)
                w0 = [(sems["v"], ptev_inc[(0, qc, 3)]),
                      (sems["v"], ptev_inc[(1, qc, 3)])]
                if last_pve_l[0] is not None:
                    w0.append((sems["v"], last_pve_l[0]))
                last = None
                for kc in range(NQC):
                    vA = A_v[:, 1024 * kc + 64 * (2 * p_i):1024 * kc + 64 * (2 * p_i) + 64]
                    vB = A_v[:, 1024 * kc + 64 * (2 * p_i) + 64:1024 * kc + 64 * (2 * p_i) + 128]
                    T(lambda e, vA=vA, kc=kc, slot=slot: e.matmul(
                        ps[0:64, 2048 + slot:2048 + slot + 128], vA,
                        U_PT[0][:, 128 * kc:128 * kc + 128],
                        start=(kc == 0), stop=(kc == NQC - 1), tile_position=(0, 0)),
                      waits=(w0 if kc == 0 else []))
                    last = T(lambda e, vB=vB, kc=kc, slot=slot: e.matmul(
                        ps[64:128, 3072 + slot:3072 + slot + 128], vB,
                        U_PT[1][:, 128 * kc:128 * kc + 128],
                        start=(kc == 0), stop=(kc == NQC - 1), tile_position=(0, 64)),
                        inc=(kc == NQC - 1))
                pv_inc[qc] = last

            def emit_pvev(qc):
                slot = 128 * (qc % 4)
                col = SEQ * p_i + 128 * qc
                V(lambda e, slot=slot, col=col: e.tensor_copy(
                    A_a[0:64, col:col + 128],
                    ps[0:64, 2048 + slot:2048 + slot + 128]),
                  waits=[(sems["t"], pv_inc[qc])])
                pve_inc[qc] = V(lambda e, slot=slot, col=col: e.tensor_copy(
                    A_a[64:128, col:col + 128],
                    ps[64:128, 3072 + slot:3072 + slot + 128]))
                last_pve_l[0] = pve_inc[qc]

            last_pve_l = [last_pve]
            bm_inc = {}
            NQ_ = QCX[0]
            for qc in range(NQ_ + 1):
                if qc < NQ_:
                    emit_qk(0, qc)
                    emit_bias_max(0, qc)
                    emit_qk(1, qc)
                    emit_bias_max(1, qc)
                    emit_exp(0, qc)
                if qc >= 1:
                    emit_l_D(0, qc - 1)
                    emit_pt(0, qc - 1)
                    emit_l_D(1, qc - 1)
                if qc < NQ_:
                    emit_exp(1, qc)
                if qc >= 1:
                    emit_pt(1, qc - 1)
                    emit_pv(qc - 1)
                    emit_pvev(qc - 1)
            pv_last = pv_inc[NQ_ - 1]
            last_pve = last_pve_l[0]
        fence()

        if DUMP[0] and s == 0:
            dma_y(d_attn, A_a)
            dma_y(d_ltmp, ltmp)
            dma_y(d_ud, U[:, 12288:12800])
            dma_y(d_up, U[:, 0:8192])
        if PH[0] < 3:
            dma_g(out2[s, 0:128, :], A_a[:, 0:1024])
            plan["y"].append(lambda e: None)
            break
        x_ready = load_xfm()
        fence()
        x2pre = A_v
        wT_ev = {}
        wT_last = {}
        mm_last = {}
        bank_ev = {}
        last_proj_ev = None
        for mo in range(8):
            sl = mo % 2
            w0 = [(sems["t"], wT_last[mo - 2])] if mo >= 2 else []
            g_w = dma_g(wnat[sl], dti["projw"][128 * mo:128 * mo + 128, :], waits=w0)
            lw = None
            for kd in range(8):
                w1_ = []
                if kd == 0:
                    w1_.append(g_w)
                    if mo >= 2:
                        w1_.append((sems["v"], wT_ev[mo - 2]))
                    if mo == 0:
                        w1_.append((sems["v"], x_ready))
                lw = T(lambda e, sl=sl, kd=kd: e.transpose(
                    psb[:, 3072 + 128 * kd:3072 + 128 * kd + 128],
                    wnat[sl][:, 128 * kd:128 * kd + 128], cident),
                    waits=w1_, inc=(kd == 7))
            wev = [(sems["t"], lw)]
            if mo - 2 in mm_last:
                wev.append((sems["t"], mm_last[mo - 2]))
            ev1 = V(lambda e, sl=sl: e.tensor_copy(
                wT[sl], psb[:, 3072:4096]), waits=wev)
            wT_ev[mo] = ev1
            wT_last[mo] = lw
            for nt in range(NKT):
                bank = 512 * (nt % 2)
                w0 = [(sems["v"], ev1)]
                if (nt % 2) in bank_ev:
                    w0.append((sems["v"], bank_ev[nt % 2]))
                last = None
                for kd in range(8):
                    last = T(lambda e, bank=bank, kd=kd, sl=sl, nt=nt: e.matmul(
                        ps[:, bank:bank + 512],
                        wT[sl][:, 128 * kd:128 * kd + 128],
                        A_a[:, SEQ * kd + 512 * nt:SEQ * kd + 512 * nt + 512],
                        start=(kd == 0), stop=(kd == 7)),
                        waits=(w0 if kd == 0 else []), inc=(kd == 7))
                mm_last[mo] = last
                dst = x2pre[:, SEQ * mo + 512 * nt:SEQ * mo + 512 * nt + 512]
                ev = V(lambda e, d=dst, bank=bank, mo=mo, nt=nt: e.scalar_tensor_tensor(
                    d, ps[:, bank:bank + 512], projbc[:, mo:mo + 1],
                    U_x[:, SEQ * mo + 512 * nt:SEQ * mo + 512 * nt + 512],
                    ALU.add, ALU.add),
                    waits=[(sems["t"], last)])
                bank_ev[nt % 2] = ev
                last_proj_ev = ev
        fence()

        if DUMP[0] and s == 0:
            dma_y(d_x2pre, A_v)
        def emit_ln(src_, dst, to_out=False):
            bank_free = {}
            sq_slot_mm = {}
            bca = {}
            for nt in range(NKT):
                lastx = None
                for kd in range(8):
                    w0 = []
                    if kd == 0 and nt >= 1:
                        w0.append((sems["v"], bca[(1, nt - 1)]))
                    lastx = T(lambda e, nt=nt, kd=kd, src=src_: e.matmul(
                        ps[32:33, 0:512], ones128,
                        src_[:, SEQ * kd + 512 * nt:SEQ * kd + 512 * nt + 512],
                        start=(kd == 0), stop=(kd == 7), tile_position=(0, 32)),
                        waits=w0, inc=(kd == 7))
                lasts = None
                for kd in range(8):
                    sslot = rowsb[:, 512 * (kd % 2):512 * (kd % 2) + 512]
                    wv = []
                    if (kd % 2) in sq_slot_mm:
                        wv.append((sems["t"], sq_slot_mm[kd % 2]))
                    sqe = V(lambda e, kd=kd, nt=nt, src=src_, sl_=sslot: e.tensor_tensor(
                        sl_, src_[:, SEQ * kd + 512 * nt:SEQ * kd + 512 * nt + 512],
                        src_[:, SEQ * kd + 512 * nt:SEQ * kd + 512 * nt + 512],
                        ALU.mult), waits=wv)
                    lasts = T(lambda e, kd=kd, sl_=sslot: e.matmul(
                        ps[32:33, 512:1024], ones128, sl_,
                        start=(kd == 0), stop=(kd == 7), tile_position=(0, 32)),
                        waits=[(sems["v"], sqe)], inc=True)
                    sq_slot_mm[kd % 2] = lasts
                evx = V(lambda e: e.tensor_copy(rowsf[32:33, 0:512], ps[32:33, 0:512]),
                        waits=[(sems["t"], lastx)])
                evs = V(lambda e: e.tensor_copy(rowsf[32:33, 512:1024],
                                                ps[32:33, 512:1024]),
                        waits=[(sems["t"], lasts)])
                V(lambda e: e.tensor_scalar(rowsf[32:33, 0:512], rowsf[32:33, 0:512],
                                            1.0 / D, None, ALU.mult))
                V(lambda e: e.tensor_tensor(rowsf[32:33, 1024:1536],
                                            rowsf[32:33, 0:512],
                                            rowsf[32:33, 0:512], ALU.mult))
                V(lambda e: e.scalar_tensor_tensor(
                    rowsf[32:33, 512:1024], rowsf[32:33, 512:1024], 1.0 / D,
                    rowsf[32:33, 1024:1536], ALU.mult, ALU.subtract))
                V(lambda e: e.tensor_scalar(rowsf[32:33, 512:1024],
                                            rowsf[32:33, 512:1024],
                                            1e-5, None, ALU.add))
                rc = V(lambda e: e.reciprocal(rowsf[32:33, 512:1024],
                                              rowsf[32:33, 512:1024]))
                st = S(lambda e: e.activation(rowsf[32:33, 512:1024],
                                              rowsf[32:33, 512:1024], ACT.Sqrt),
                       waits=[(sems["v"], rc)])
                for ridx in (0, 1):
                    w0 = [(sems["v"], rc)] if ridx == 0 else [(sems["s"], st)]
                    if (ridx, nt - 1) in bca:
                        w0.append((sems["v"], bca[(ridx, nt - 1)]))
                    lastb = T(lambda e, nt=nt, ridx=ridx: e.matmul(
                        ps[:, 1024 + 512 * ridx:1024 + 512 * ridx + 512],
                        onescol[32:33, :], rowsf[32:33, 512 * ridx:512 * ridx + 512],
                        start=True, stop=True), waits=w0, inc=True)
                    bca[(ridx, nt)] = V(lambda e, ridx=ridx, nt=nt: e.tensor_copy(
                        A_bc[:, SEQ * ridx + 512 * nt:SEQ * ridx + 512 * nt + 512],
                        ps[:, 1024 + 512 * ridx:1024 + 512 * ridx + 512]),
                        waits=[(sems["t"], lastb)])
            meanb = A_bc[:, 0:SEQ]
            rstdb = A_bc[:, SEQ:2 * SEQ]
            wlast = (sems["v"], bca[(1, NKT - 1)])
            if not to_out:
                for kd in range(8):
                    V(lambda e, kd=kd, src=src_: e.tensor_tensor(
                        rowsb[:, 0:SEQ], src[:, SEQ * kd:SEQ * kd + SEQ],
                        meanb, ALU.subtract),
                      waits=([wlast] if kd == 0 else []))
                    V(lambda e, kd=kd, dst=dst: e.tensor_tensor(
                        dst[:, SEQ * kd:SEQ * kd + SEQ], rowsb[:, 0:SEQ],
                        rstdb, ALU.mult))
            else:
                prev_dma = {}
                prev_oev = {}
                prev_tp = {}
                for tc in range(NQC):
                    otslot = A_ot[:, 1024 * (tc % 2):1024 * (tc % 2) + 1024]
                    oev = None
                    for kd in range(8):
                        w0 = []
                        if kd == 0:
                            w0.append(wlast)
                            if tc - 2 in prev_dma:
                                w0.append(prev_dma[tc - 2])
                        ofa = A_of[:, 256 * (kd % 2):256 * (kd % 2) + 128]
                        ofb = A_of[:, 256 * (kd % 2) + 128:256 * (kd % 2) + 256]
                        if (tc, kd - 2) in prev_tp:
                            w0.append((sems["t"], prev_tp[(tc, kd - 2)]))
                        elif kd < 2 and (tc - 1, kd + 6) in prev_tp:
                            w0.append((sems["t"], prev_tp[(tc - 1, kd + 6)]))
                        V(lambda e, kd=kd, tc=tc, src=src_, ofa=ofa: e.tensor_tensor(
                            ofa, src[:, SEQ * kd + 128 * tc:SEQ * kd + 128 * tc + 128],
                            meanb[:, 128 * tc:128 * tc + 128], ALU.subtract),
                          waits=w0)
                        ofv = V(lambda e, kd=kd, tc=tc, ofa=ofa, ofb=ofb: e.tensor_tensor(
                            ofb, ofa, rstdb[:, 128 * tc:128 * tc + 128], ALU.mult))
                        tpw = [(sems["v"], ofv)]
                        if (tc, kd - 2) in prev_oev:
                            tpw.append((sems["v"], prev_oev[(tc, kd - 2)]))
                        elif kd < 2 and (tc - 1, kd + 6) in prev_oev:
                            tpw.append((sems["v"], prev_oev[(tc - 1, kd + 6)]))
                        tp = T(lambda e, kd=kd, ofb=ofb: e.transpose(
                            ps[:, 3072 + 512 * (kd % 2):3072 + 512 * (kd % 2) + 128],
                            ofb, cidentf), waits=tpw, inc=True)
                        prev_tp[(tc, kd)] = tp
                        oev = V(lambda e, kd=kd, ot=otslot: e.tensor_copy(
                            ot[:, 128 * kd:128 * kd + 128],
                            ps[:, 3072 + 512 * (kd % 2):3072 + 512 * (kd % 2) + 128]),
                            waits=[(sems["t"], tp)])
                        prev_oev[(tc, kd)] = oev
                    prev_dma[tc] = dma_y(out2[s, 128 * tc:128 * tc + 128, :], otslot,
                                         waits=[(sems["v"], oev)])

        emit_ln(x2pre, A_a)
        fence()
        if DUMP[0] and s == 0:
            dma_y(d_x1, A_a)

        if PH[0] < 4 and PH[0] not in (31, 32, 33):
            dma_g(out2[s, 0:128, :], A_a[:, 0:1024])
            break
        x1 = A_a
        x3pre = A_v
        g_rw = dma_g(rowsb[0:8, 0:D], dti["rw"])
        lastr = None
        for kd in range(8):
            lastr = T(lambda e, kd=kd: e.transpose(
                psb[:, 8 * kd:8 * kd + 8],
                rowsb[0:8, 128 * kd:128 * kd + 128], cident[0:8, 0:8]),
                waits=([g_rw] if kd == 0 else []), inc=(kd == 7))
        rwT = A_w[:, 0:64]
        evrw = V(lambda e: e.tensor_copy(rwT, psb[:, 0:64]),
            waits=[(sems["t"], lastr)])
        lastl = None
        for nt in range(NKT):
            for kd in range(8):
                lastl = T(lambda e, nt=nt, kd=kd: e.matmul(
                    ps[32:40, 2048 + 512 * nt:2048 + 512 * nt + 512],
                    rwT[:, 8 * kd:8 * kd + 8],
                    x1[:, SEQ * kd + 512 * nt:SEQ * kd + 512 * nt + 512],
                    start=(kd == 0), stop=(kd == 7), tile_position=(0, 32)),
                    waits=([(sems["v"], evrw)] if kd == 0 and nt == 0 else []),
                    inc=(kd == 7 and nt == NKT - 1))
        evl = V(lambda e: e.tensor_scalar(rowsf[32:40, 0:SEQ],
                                          ps[32:40, 2048:2048 + SEQ],
                                          ebias_c[32:40, 0:1], None, ALU.add),
                waits=[(sems["t"], lastl)])
        prev_ct = {}
        for tc in range(NQC):
            w0 = [(sems["v"], evl)] if tc == 0 else []
            if tc - 2 in prev_ct:
                w0.append((sems["v"], prev_ct[tc - 2]))
            ltc = 2048 + 1024 * (tc % 2)
            lt = T(lambda e, tc=tc, ltc=ltc: e.transpose(
                ps[0:128, ltc:ltc + 8],
                rowsf[32:40, 128 * tc:128 * tc + 128], cidentf[32:40, 32:40]),
                waits=w0, inc=True)
            lg = smallf[:, 0:8]
            m1 = smallf[:, 8:9]
            m2 = smallf[:, 9:10]
            keep = smallf[:, 10:18]
            spv = smallf[:, 18:26]
            ev1 = V(lambda e, ltc=ltc: e.tensor_copy(
                lg, ps[:, ltc:ltc + 8]),
                waits=[(sems["t"], lt), (sems["s"], s2_prev[0])] if s2_prev[0]
                else [(sems["t"], lt)])
            psl = ps[:, ltc:ltc + 8]
            sp_ = lambda e: e.memset(smallf[:, 32:48], 0.0)
            V(sp_)
            V(lambda e, psl=psl: e.reduce_max(m1, psl, axis=AX))
            for _ in range(6):
                V(sp_)
            V(lambda e, psl=psl: e.tensor_scalar(keep, psl, m1, None, ALU.is_ge))
            V(sp_)
            V(sp_)
            V(lambda e, psl=psl: e.scalar_tensor_tensor(
                spv, keep, -1e30, psl, ALU.mult, ALU.add))
            V(sp_)
            V(sp_)
            V(lambda e: e.reduce_max(m2, spv, axis=AX))
            for _ in range(6):
                V(sp_)
            prev_ct[tc] = V(lambda e, psl=psl: e.tensor_scalar(
                keep, psl, m2, None, ALU.is_ge))
            s0 = S(lambda e: e.activation(spv, lg, ACT.Abs),
                   waits=[(sems["v"], sems["v"].n)])
            s1 = S(lambda e: e.activation(spv, spv, ACT.Exp, scale=-1.0))
            s2 = S(lambda e: e.activation(spv, spv, ACT.Ln, bias=1.0))
            s2_prev[0] = s2
            V(lambda e: e.tensor_scalar(lg, lg, 0.0, None, ALU.max),
              waits=[(sems["s"], s2)])
            V(sp_)
            V(lambda e: e.tensor_tensor(lg, lg, spv, ALU.add))
            V(sp_)
            V(lambda e, tc=tc: e.tensor_tensor(
                smallf[:, 64 + 8 * tc:64 + 8 * tc + 8], lg, keep, ALU.mult))
        sq2 = S(lambda e: e.activation(smallf[:, 64:192], smallf[:, 64:192], ACT.Sqrt),
                waits=[(sems["v"], sems["v"].n)])
        prev_cf = {}
        for tc in range(NQC):
            w0 = [(sems["s"], sq2)] if tc == 0 else []
            if tc - 2 in prev_cf:
                w0.append((sems["v"], prev_cf[tc - 2]))
            ctc = 2048 + 1024 * (tc % 2)
            ct = T(lambda e, tc=tc, ctc=ctc: e.transpose(
                ps[0:8, ctc:ctc + 128],
                smallf[:, 64 + 8 * tc:64 + 8 * tc + 8], cidentf),
                waits=w0, inc=True)
            prev_cf[tc] = V(lambda e, tc=tc, ctc=ctc: e.tensor_copy(
                rowsf[0:8, 128 * tc:128 * tc + 128],
                ps[0:8, ctc:ctc + 128]),
                waits=[(sems["t"], ct)])
        cf_last = prev_cf[NQC - 1]
        ms1 = V(lambda e: e.memset(rowsb[0:32, 0:SEQ], 1.0),
                waits=[(sems["v"], cf_last)])
        cvb = V(lambda e: e.tensor_copy(rowsb[0:8, 0:SEQ], rowsf[0:8, 0:SEQ]),
                waits=[(sems["v"], ms1)])
        g_cfd = dma_g(cfdram, rowsf[0:8, 0:SEQ], waits=[(sems["v"], cf_last)])
        biasm = U[:, 12288:13312]
        gb = None
        for e_i in range(8):
            gb = dma_g(biasm[e_i:e_i + 1, 0:D], dti["rb2"][e_i:e_i + 1, :])
        gb = dma_g(biasm[8:9, 0:D], dti["sb2"].rearrange("(o d) -> o d", o=1))
        if DUMP[0] and s == 0:
            dma_y(d_sf, smallf[:, 0:192], waits=[(sems["v"], sems["v"].n)])
            dma_y(d_lg, rowsf[32:40, 0:SEQ], waits=[(sems["v"], evl)])
        fence()

        w1stage = [A_tab[:, 0:4096], U[:, 8192:12288]]
        if PH[0] == 31:
            dma_g(out2[s, 0:128, :], A_a[:, 0:1024])
            break
        yev = {}
        for half in range(2):
            tok0 = 1024 * half
            g_w1 = {}
            g_cb = {}
            sil_done = {}
            scl_done = {}
            lasth = {}
            for e_i in range(NE):
                sl = e_i % 2
                w0 = [precast_done]
                if e_i >= 2:
                    w0.append((sems["t"], lasth[e_i - 2]))
                if w1_free[sl] is not None:
                    w0.append((sems["t"], w1_free[sl]))
                if DUMP[0] and dump_tok[0] is not None:
                    w0.append(dump_tok[0])
                g_w1[e_i] = dma_g(w1stage[sl], w1bf[e_i, :, :], waits=w0)
                if e_i < 8:
                    cb = A_cb[:, 1024 * sl:1024 * sl + 1024]
                    wcb = [g_cfd]
                    if cb_free[sl] is not None:
                        wcb.append((sems["v"], cb_free[sl]))
                    g_cb[e_i] = dma_g(
                        cb, cfdram[e_i:e_i + 1, tok0:tok0 + 1024]
                        .partition_broadcast(128), waits=wcb)
                for sub in range(2):
                    t0 = tok0 + 512 * sub
                    pb = 2048 * sub
                    lh = None
                    for mh in range(4):
                        for kd in range(8):
                            w2_ = []
                            if mh == 0 and kd == 0:
                                w2_.append(g_w1[e_i])
                                if (e_i - 1, sub) in sil_done:
                                    w2_.append((sems["s"], sil_done[(e_i - 1, sub)]))
                                if e_i == 0 and (2 + sub, 3) in yev:
                                    w2_.append((sems["v"], yev[(2 + sub, 3)]))
                            lh = T(lambda e, sl=sl, mh=mh, kd=kd, t0=t0, pb=pb:
                                   e.matmul(
                                       ps[:, pb + 512 * mh:pb + 512 * mh + 512],
                                       w1stage[sl][:, 512 * kd + 128 * mh:512 * kd + 128 * mh + 128],
                                       x1[:, SEQ * kd + t0:SEQ * kd + t0 + 512],
                                       start=(kd == 0), stop=(kd == 7)),
                                   waits=w2_, inc=(kd == 7 and mh == 3))
                    lasth[e_i] = lh
                    w1_free[sl] = lh
                    hv = htil_view(e_i)
                    ssil = None
                    for mh in range(4):
                        ssil = S(lambda e, hv=hv, mh=mh, sub=sub, e_i=e_i, pb=pb:
                                 e.activation(
                                     hv[:, 1024 * mh + 512 * sub:1024 * mh + 512 * sub + 512],
                                     ps[:, pb + 512 * mh:pb + 512 * mh + 512],
                                     ACT.Silu, bias=rb1c[:, 4 * e_i + mh:4 * e_i + mh + 1]),
                                 waits=([(sems["t"], lh)] if mh == 0 else []))
                    sil_done[(e_i, sub)] = ssil
                    if e_i < 8:
                        sc = None
                        for mh in range(4):
                            w3_ = []
                            if mh == 0:
                                w3_ = [(sems["s"], ssil), g_cb[e_i]]
                            sc = V(lambda e, hv=hv, mh=mh, sub=sub, cb=cb: e.tensor_tensor(
                                hv[:, 1024 * mh + 512 * sub:1024 * mh + 512 * sub + 512],
                                hv[:, 1024 * mh + 512 * sub:1024 * mh + 512 * sub + 512],
                                cb[:, 512 * sub:512 * sub + 512], ALU.mult), waits=w3_)
                        scl_done[(e_i, sub)] = sc
                        cb_free[sl] = sc
                    else:
                        scl_done[(e_i, sub)] = V(
                            lambda e: e.memset(smallf[0:1, 190:191], 0.0),
                            waits=[(sems["s"], ssil)])
            if PH[0] == 32:
                continue
            yev = {}
            for sub in range(2):
                t0 = tok0 + 512 * sub
                for grp in range(2):
                    pbase = 2048 * ((2 * sub + grp) % 2)
                    dum = None
                    wd = [(sems["s"], sil_done[(NE - 1, 1)]),
                          (sems["v"], scl_done[(7, 1)])]
                    if (sub, grp) != (0, 0):
                        pass
                    for mdl in range(4):
                        if YMODE[0] == 2 and mdl > 0:
                            continue
                        w5_ = list(wd) if mdl == 0 else []
                        if mdl == 0 and ((2 * sub + grp) - 2, 3) in yev:
                            w5_.append((sems["v"], yev[((2 * sub + grp) - 2, 3)]))
                        if YMODE[0] == 2:
                            for mdl2 in range(4):
                                if ((2 * sub + grp) - 2, mdl2) in yev:
                                    w5_.append((sems["v"], yev[((2 * sub + grp) - 2, mdl2)]))
                            dum = T(lambda e: e.nop(), waits=w5_, inc=True)
                            continue
                        dum = T(lambda e, pbase=pbase, mdl=mdl: e.matmul(
                            ps[:, pbase + 512 * mdl:pbase + 512 * mdl + 512],
                            zrow[0:1, 0:128], zrow, start=True, stop=False),
                            waits=w5_, inc=(mdl == 3))
                    lasty = {}
                    for e_i in range(NE):
                        if YONLY[0] and e_i > 0:
                            lasty[e_i] = lasty[e_i - 1]
                            continue
                        sl2 = e_i % 2
                        w7_ = [precast_done]
                        if e_i >= 2:
                            w7_.append((sems["t"], lasty[e_i - 2]))
                        if w2_free[sl2] is not None:
                            w7_.append((sems["t"], w2_free[sl2]))
                        gw2 = dma_g(U_w2db[sl2].rearrange("p (kh c) -> p kh c", kh=4),
                                    w2bf[e_i, :, :].rearrange(
                                        "p (kh c) -> p kh c", kh=4)[:, :, 512 * grp:512 * grp + 512],
                                    waits=w7_)
                        ly = None
                        for mdl in range(4):
                            md = 4 * grp + mdl
                            hv = htil_view(e_i)
                            for kh in range(4):
                                w4_ = []
                                if mdl == 0 and kh == 0:
                                    w4_.append(gw2)
                                    if e_i == 0:
                                        w4_.append((sems["t"], dum))
                                st_ = (YMODE[0] == 2 and e_i == 0 and kh == 0)
                                sp_ = (YMODE[0] == 2 and e_i == 0 and kh == 3)
                                ly = T(lambda e, sl2=sl2, mdl=mdl, kh=kh, hv=hv,
                                       sub=sub, pbase=pbase, st_=st_, sp_=sp_: e.matmul(
                                           ps[:, pbase + 512 * mdl:pbase + 512 * mdl + 512],
                                           U_w2db[sl2][:, 512 * kh + 128 * mdl:512 * kh + 128 * mdl + 128],
                                           hv[:, 1024 * kh + 512 * sub:1024 * kh + 512 * sub + 512],
                                           start=st_, stop=sp_),
                                       waits=w4_, inc=(mdl == 3 and kh == 3))
                        lasty[e_i] = ly
                        w2_free[sl2] = ly
                    lb = None
                    for mdl in range(4):
                        if YMODE[0] == 2:
                            lb = lasty[NE - 1]
                            break
                        md = 4 * grp + mdl
                        w6_ = [gb, (sems["v"], cvb)] if mdl == 0 else []
                        lb = T(lambda e, md=md, mdl=mdl, t0=t0, pbase=pbase: e.matmul(
                            ps[:, pbase + 512 * mdl:pbase + 512 * mdl + 512],
                            biasm[0:9, 128 * md:128 * md + 128],
                            rowsb[0:9, t0:t0 + 512], start=False, stop=True),
                            waits=w6_, inc=(mdl == 3))
                    if DUMP[0] and s == 0 and half == 0 and sub == 0 and grp == 0:
                        vv = V(lambda e: e.tensor_copy(rowsf[:, 0:2048].bitcast(F32) if False else rowsf[:, 0:2048], ps[:, 0:2048]),
                               waits=[(sems["t"], lb)])
                        dma_y(d_y, rowsf[:, 0:2048], waits=[(sems["v"], vv)])
                    for mdl in range(4):
                        md = 4 * grp + mdl
                        yev[((2 * sub + grp), mdl)] = V(
                            lambda e, md=md, mdl=mdl, t0=t0, pbase=pbase:
                            e.scalar_tensor_tensor(
                                x3pre[:, SEQ * md + t0:SEQ * md + t0 + 512],
                                ps[:, pbase + 512 * mdl:pbase + 512 * mdl + 512], 1.0,
                                x1[:, SEQ * md + t0:SEQ * md + t0 + 512],
                                ALU.mult, ALU.add),
                            waits=[(sems["t"], lb)])
            if DUMP[0] and s == 0 and half == 0:
                dump_tok[0] = dma_y(d_ht0, A_q[:, 0:4096],
                                    waits=[(sems["t"], sems["t"].n)])
        fence()

        if DUMP[0] and s == 0:
            dma_y(d_x3pre, A_v)
            dma_y(d_cf, cfdram)
            dma_y(d_ht, A_q[:, 0:4096])
            dma_y(d_w2, U[:, 4096:8192])
        if PH[0] < 5:
            dma_g(out2[s, 0:128, :], A_v[:, 0:1024])
            break
        emit_ln(x3pre, None, to_out=True)
        fence()

    dfin = [(c, c.n) for k in "yg" for c in dpool[k]]
    def _fin(eng, dfin=dfin):
        for (c, v) in dfin:
            if v > 0:
                eng.wait_ge(c.sem, v)
    plan["y"].append(_fin)

    with nc.Block() as blk:
        @blk.tensor
        def _(eng):
            for f in plan["t"]:
                f(eng)
        @blk.vector
        def _(eng):
            for f in plan["v"]:
                f(eng)
        @blk.scalar
        def _(eng):
            for f in plan["s"]:
                f(eng)
        @blk.sync
        def _(eng):
            for f in plan["y"]:
                f(eng)
        @blk.gpsimd
        def _(eng):
            for f in plan["g"]:
                f(eng)
    return nc


s2_prev = [None]


def kernel(L0_attn_x, L0_attn_qkv_w, L0_attn_proj_w, L0_attn_proj_b, L0_attn_slopes,
           L0_ffn_routing_W, L0_ffn_expert_bias, L0_ffn_sW1, L0_ffn_sb1, L0_ffn_sW2,
           L0_ffn_sb2, L0_ffn_rW1, L0_ffn_rb1, L0_ffn_rW2, L0_ffn_rb2):
    s2_prev[0] = None
    x = np.ascontiguousarray(np.asarray(L0_attn_x, dtype=np.float32))
    consts = _host_consts()
    nc = build_program()
    common = {
        "qkvw": np.ascontiguousarray(np.asarray(L0_attn_qkv_w, np.float32)),
        "projw": np.ascontiguousarray(np.asarray(L0_attn_proj_w, np.float32)),
        "projb": np.ascontiguousarray(np.asarray(L0_attn_proj_b, np.float32)),
        "slopes": np.ascontiguousarray(np.asarray(L0_attn_slopes, np.float32)),
        "rw": np.ascontiguousarray(np.asarray(L0_ffn_routing_W, np.float32)),
        "ebias": np.ascontiguousarray(np.asarray(L0_ffn_expert_bias, np.float32)),
        "sw1": np.ascontiguousarray(np.asarray(L0_ffn_sW1, np.float32)),
        "sb1": np.ascontiguousarray(np.asarray(L0_ffn_sb1, np.float32)),
        "sw2": np.ascontiguousarray(np.asarray(L0_ffn_sW2, np.float32)),
        "sb2": np.ascontiguousarray(np.asarray(L0_ffn_sb2, np.float32)),
        "rw1": np.ascontiguousarray(np.asarray(L0_ffn_rW1, np.float32)),
        "rb1": np.ascontiguousarray(np.asarray(L0_ffn_rb1, np.float32)),
        "rw2": np.ascontiguousarray(np.asarray(L0_ffn_rW2, np.float32)),
        "rb2": np.ascontiguousarray(np.asarray(L0_ffn_rb2, np.float32)),
    }
    common.update(consts)
    in_maps = []
    for c in range(NCORES):
        m = dict(common)
        m["x"] = np.ascontiguousarray(x[2 * c:2 * c + 2].reshape(NSEQ * SEQ, D))
        in_maps.append(m)
    res = run_bass_kernel_spmd(nc, in_maps, core_ids=list(range(NCORES)))
    outs = [np.asarray(res.results[c]["out"]).reshape(NSEQ, SEQ, D)
            for c in range(NCORES)]
    return np.concatenate(outs, axis=0).astype(np.float32)
